# version 11
# baseline (speedup 1.0000x reference)
import numpy as np
import ml_dtypes
import concourse.bass as bass
import concourse.mybir as mybir
from concourse.bass_utils import run_bass_kernel_spmd
from concourse.bass_core_rust import IndirectOffsetOnAxis
from contextlib import ExitStack

F32 = mybir.dt.float32
BF16 = mybir.dt.bfloat16
AF = mybir.ActivationFunctionType
ALU = mybir.AluOpType
AX = mybir.AxisListType

B, T, D = 32, 2048, 1024
NCORES = 8
NTOK = (B // NCORES) * T
NH, HD = 16, 64
E = 8
HDN = 4096
CAP = 16384
NEG = -480.0
MOE_V1 = True
ESLOT = 16384


def build_program(dbg=(), phases=4, p1sub=4, detect_races=False, serial=0):
    nc = bass.Bass("TRN2", target_bir_lowering=False,
                   detect_race_conditions=detect_races)

    x_io = nc.dram_tensor("x", [NTOK, D], F32, kind="ExternalInput").ap()
    wq_io = nc.dram_tensor("wq", [128, 8 * 1024], F32, kind="ExternalInput").ap()
    wkvg_io = nc.dram_tensor("wkvg", [128, 8 * 136], F32, kind="ExternalInput").ap()
    wo_io = nc.dram_tensor("wo", [128, 8 * 1024], F32, kind="ExternalInput").ap()
    bq_io = nc.dram_tensor("bq", [1, 1024], F32, kind="ExternalInput").ap()
    bkv_io = nc.dram_tensor("bkv", [1, 136], F32, kind="ExternalInput").ap()
    bo_io = nc.dram_tensor("bo", [128, 8], F32, kind="ExternalInput").ap()
    EW = 1 if MOE_V1 else E
    w1_io = nc.dram_tensor("w1", [EW, 128, 8 * HDN], F32, kind="ExternalInput").ap()
    w2_io = nc.dram_tensor("w2", [EW, 128, 32 * 1024], F32, kind="ExternalInput").ap()
    b1_io = nc.dram_tensor("b1", [EW, 128, 32], F32, kind="ExternalInput").ap()
    b2_io = nc.dram_tensor("b2", [EW, 128, 8], F32, kind="ExternalInput").ap()
    b1r_io = nc.dram_tensor("b1r", [1, HDN], F32, kind="ExternalInput").ap()
    b2r_io = nc.dram_tensor("b2r", [1, D], F32, kind="ExternalInput").ap()
    cos_io = nc.dram_tensor("cosT", [128, 1024], F32, kind="ExternalInput").ap()
    sin_io = nc.dram_tensor("sinT", [128, 1024], F32, kind="ExternalInput").ap()
    ident_io = nc.dram_tensor("identc", [128, 128], F32, kind="ExternalInput").ap()
    tri_io = nc.dram_tensor("tri", [128, 4 * 512], F32, kind="ExternalInput").ap()
    ltri_io = nc.dram_tensor("ltri", [128, 128], F32, kind="ExternalInput").ap()
    selz_io = nc.dram_tensor("selz", [16, 8 * 128], F32, kind="ExternalInput").ap()
    selb_io = nc.dram_tensor("selb", [128, 16], F32, kind="ExternalInput").ap()
    iotar_io = nc.dram_tensor("iotar", [128, 8], F32, kind="ExternalInput").ap()
    gidx_io = nc.dram_tensor("gidx", [128, 64], F32, kind="ExternalInput").ap()
    ebase_io = nc.dram_tensor("ebase", [128, 128],
                          mybir.dt.int32, kind="ExternalInput").ap()

    out_io = nc.dram_tensor("out", [NTOK, D], F32, kind="ExternalOutput").ap()

    xnT_sc = nc.dram_tensor("xnT_sc", [8, 16, 128, 512], BF16).ap()
    attn_sc = nc.dram_tensor("attn_sc", [NTOK, D], BF16).ap()
    ffn_sc = nc.dram_tensor("ffn_sc", [8, 16, 128, 512], BF16).ap()
    crow_sc = nc.dram_tensor("crow_sc", [E, 64, 128], BF16).ap()
    xnrow_sc = nc.dram_tensor("xnrow_sc", [NTOK, D], BF16).ap()
    xn_all = nc.dram_tensor("xn_all", [8 * NTOK, D], BF16).ap()
    buck_loc = nc.dram_tensor("buck_loc", [16 * ESLOT, 2], F32).ap()
    buck_all = nc.dram_tensor("buck_all", [8 * ESLOT, 2], F32).ap()
    ycon = nc.dram_tensor("ycon", [8 * NTOK + 128, D], BF16).ap()
    yrs = nc.dram_tensor("yrs", [NTOK, D], BF16).ap()
    zdram = nc.dram_tensor("zdram", [128 * 8192], BF16).ap()
    w1bf = nc.dram_tensor("w1bf", [128, 8 * HDN], BF16).ap()
    w2bf = nc.dram_tensor("w2bf", [128, 32 * 1024], BF16).ap()
    base_bounce = nc.dram_tensor("base_bounce", [16, 1], F32).ap()
    ag_in = nc.dram_tensor("ag_in", [16, 1], F32).ap()
    ag_out = nc.dram_tensor("ag_out", [128, 1], F32).ap()

    dbg_outs = {}

    def dbgt(name, shape):
        if name in dbg:
            dbg_outs[name] = nc.dram_tensor("dbg_" + name, shape, F32,
                                            kind="ExternalOutput").ap()
        return dbg_outs.get(name)

    d_lall = dbgt("lall", [128, 512])
    d_route = dbgt("route", [128, 8 * 64])
    d_cnt = dbgt("cnt", [128, 1])
    d_attn = dbgt("attn", [NTOK, D])
    d_xn = dbgt("xn", [8 * 16 * 128, 512])

    es = ExitStack()
    with es:
        en = es.enter_context

        sems = {}
        cnts = {}
        for nm in ["ld", "cst", "sto", "outs", "pe", "dve", "act", "cc",
                   "ldx0", "ldx1", "xns0", "xns1", "stg0", "stg1",
                   "aro0", "aro1", "ktd", "ldg0", "ldg1", "ldf0", "ldf1",
                   "out0", "out1", "agl", "vch", "sch",
                   "xnr0", "xnr1", "bsc", "ccx", "ccy", "ccb",
                   "pg0", "pg1", "ct0", "ct1", "xga", "xgb",
                   "ym0", "ym1", "ycz", "wbf", "wld",
                   "m_pe", "m_dve", "m_act", "m_gp", "m_sy"]:
            sems[nm] = en(nc.semaphore(nm))
            cnts[nm] = 0

        def inc(instr, nm, k=1):
            cnts[nm] += k
            return instr.then_inc(sems[nm], k)

        def n(nm):
            return cnts[nm]

        def wt(eng, nm, val=None):
            eng.wait_ge(sems[nm], cnts[nm] if val is None else val)

        def cs(eng, ins, nm="vch"):
            cnts[nm] += 1
            ins.then_inc(sems[nm], 1)
            eng.wait_ge(sems[nm], cnts[nm])

        last_use = {}

        def use_res(eng, *names):
            for nm in names:
                for s, v in last_use.get(nm, ()):
                    eng.wait_ge(sems[s], v)

        def set_res(name, *semnames):
            last_use[name] = [(s, cnts[s]) for s in semnames]

        def add_res(name, semname):
            last_use.setdefault(name, []).append((semname, cnts[semname]))

        blk = en(nc.Block())
        TE, VE, SC, GP, SY = blk.tensor, blk.vector, blk.scalar, blk.gpsimd, blk.sync

        def join_all():
            marks = [("tensor", "m_pe"), ("vector", "m_dve"), ("scalar", "m_act"),
                     ("gpsimd", "m_gp"), ("sync", "m_sy")]
            for engname, mnm in marks:
                def f(eng, mnm=mnm):
                    inc(eng.nop(), mnm)
                getattr(blk, engname)(f)
            tgt = {k: n(k) for k in ["m_pe", "m_dve", "m_act", "m_gp", "m_sy",
                                     "ld", "cst", "sto", "outs",
                                     "ldx0", "ldx1", "xns0", "xns1",
                                     "stg0", "stg1", "aro0", "aro1", "ktd",
                                     "ldg0", "ldg1", "ldf0", "ldf1",
                                     "out0", "out1", "agl",
                                     "xnr0", "xnr1", "bsc",
                                     "pg0", "pg1", "ct0", "ct1", "xga", "xgb",
                                     "ym0", "ym1", "ycz", "wbf", "wld",
                                     "cc", "ccx", "ccy", "ccb"]}
            for engname, _ in marks:
                def f(eng, tgt=tgt):
                    for k, v in tgt.items():
                        eng.wait_ge(sems[k], v)
                getattr(blk, engname)(f)

        ident_sb = en(nc.sbuf_tensor("ident_sb", [128, 128], BF16))
        ltri_sb = en(nc.sbuf_tensor("ltri_sb", [128, 128], F32))
        selb_sb = en(nc.sbuf_tensor("selb_sb", [128, 16], F32))
        iotar_sb = en(nc.sbuf_tensor("iotar_sb", [128, 8], F32))
        ones_r = en(nc.sbuf_tensor("ones_r", [1, 128], BF16))
        ones_rf = en(nc.sbuf_tensor("ones_rf", [1, 128], F32))
        ones_cf = en(nc.sbuf_tensor("ones_cf", [128, 1], F32))
        l_all = en(nc.sbuf_tensor("l_all", [128, 512], F32))
        i1_sb = en(nc.sbuf_tensor("i1_sb", [128, 64], F32))
        i2_sb = en(nc.sbuf_tensor("i2_sb", [128, 64], F32))
        gv1_sb = en(nc.sbuf_tensor("gv1_sb", [128, 64], F32))
        gv2_sb = en(nc.sbuf_tensor("gv2_sb", [128, 64], F32))
        gidx_sb = en(nc.sbuf_tensor("gidx_sb", [128, 64], F32))
        ebase_sb = en(nc.sbuf_tensor("ebase_sb", [128, 128], mybir.dt.int32))

        def f(g):
            for dst, src in [(ident_sb, ident_io), (ltri_sb, ltri_io),
                             (selb_sb, selb_io), (iotar_sb, iotar_io),
                             (gidx_sb, gidx_io), (ebase_sb, ebase_io)]:
                inc(g.dma_start(out=dst[:], in_=src), "cst", 16)
        GP(f)

        eps_sb = en(nc.sbuf_tensor("eps_sb", [128, 1], F32))

        def f(v):
            wt(v, "cst")
            v.memset(ones_r[:], 1.0)
            v.memset(ones_rf[:], 1.0)
            v.memset(eps_sb[:], 1e-5)
            inc(v.memset(ones_cf[:], 1.0), "dve")
        VE(f)
        consts_dve = n("dve")

        with ExitStack() as s1:
            e1 = s1.enter_context
            if MOE_V1:
                def f(g):
                    inc(g.dma_start(out=w1bf, in_=w1_io[0]), "wbf", 16)
                    inc(g.dma_start(out=w2bf, in_=w2_io[0]), "wbf", 16)
                GP(f)

                ztile = e1(nc.sbuf_tensor("ztile", [128, 1024], BF16))

                def f(v):
                    inc(v.memset(ztile[:], 0.0), "dve")
                VE(f)
                zt_done = n("dve")

                def f(sy):
                    wt(sy, "dve", zt_done)
                    for r in range(8):
                        inc(sy.dma_start(
                            out=zdram[r * 131072:(r + 1) * 131072],
                            in_=ztile[:]), "ycz", 16)
                    wt(sy, "ycz")
                    yflat = ycon.rearrange("a c -> (a c)")
                    CH = 128 * 8192
                    for r in range(64):
                        inc(sy.dma_start(out=yflat[r * CH:(r + 1) * CH],
                                         in_=zdram), "ycz", 16)
                    inc(sy.dma_start(out=yflat[64 * CH:64 * CH + 131072],
                                     in_=zdram[0:131072]), "ycz", 16)
                SY(f)
            wq_sb = e1(nc.sbuf_tensor("wq_sb", [128, 8 * 1024], BF16))
            wkvg_sb = e1(nc.sbuf_tensor("wkvg_sb", [128, 8 * 136], BF16))
            wo_sb = e1(nc.sbuf_tensor("wo_sb", [128, 8 * 1024], BF16))
            bq_sb = e1(nc.sbuf_tensor("bq_sb", [1, 1024], BF16))
            bkv_sb = e1(nc.sbuf_tensor("bkv_sb", [1, 136], BF16))
            bo_sb = e1(nc.sbuf_tensor("bo_sb", [128, 8], F32))
            cos_sb = e1(nc.sbuf_tensor("cos_sb", [128, 1024], F32))
            sin_sb = e1(nc.sbuf_tensor("sin_sb", [128, 1024], F32))
            tri_sb = e1(nc.sbuf_tensor("tri_sb", [128, 2048], BF16))
            selz_sb = e1(nc.sbuf_tensor("selz_sb", [16, 1024], BF16))
            xbuf = e1(nc.sbuf_tensor("xbuf", [128, 2 * 1024], F32))
            sq_buf = e1(nc.sbuf_tensor("sq_buf", [128, 1024], F32))
            mrs = e1(nc.sbuf_tensor("mrs", [128, 16], F32))
            xn_bf = e1(nc.sbuf_tensor("xn_bf", [128, 2 * 1024], BF16))
            xnT_t = e1(nc.sbuf_tensor("xnT_t", [128, 2 * 1024], BF16))
            q_tile = e1(nc.sbuf_tensor("q_tile", [128, 2 * 1024], BF16))
            kv_t = e1(nc.sbuf_tensor("kv_t", [128, 2 * 136], F32))
            k_seq = e1(nc.sbuf_tensor("k_seq", [128, 1152], BF16))
            kT_seq = e1(nc.sbuf_tensor("kT_seq", [128, 2048], BF16))
            stag_sb = e1(nc.sbuf_tensor("stag_sb", [128, 2 * 512], BF16))
            vaug_seq = e1(nc.sbuf_tensor("vaug_seq", [128, 16 * 65], BF16))
            qT_seq = e1(nc.sbuf_tensor("qT_seq", [128, 8 * 2048], BF16))
            pT_buf = e1(nc.sbuf_tensor("pT_buf", [128, 16 * 512], BF16))
            attnT_un = e1(nc.sbuf_tensor("attnT_un", [128, 8 * 2048], BF16))
            z_seq = e1(nc.sbuf_tensor("z_seq", [16, 2048], BF16))
            rz_seq = e1(nc.sbuf_tensor("rz_seq", [16, 2048], BF16))
            rb_sb = e1(nc.sbuf_tensor("rb_sb", [128, 2 * 512], F32))
            anorm_g = e1(nc.sbuf_tensor("anorm_g", [128, 8 * 512], BF16))
            aoutT_g = e1(nc.sbuf_tensor("aoutT_g", [128, 8 * 512], BF16))
            arow_sb = e1(nc.sbuf_tensor("arow_sb", [128, 2 * 1024], BF16))

            ps_a = e1(nc.psum_tensor("ps_a", [128, 1024], F32))
            ps_kv = e1(nc.psum_tensor("ps_kv", [128, 512], F32))
            ps_s = e1(nc.psum_tensor("ps_s", [128, 1024], F32))
            ps_pv = e1(nc.psum_tensor("ps_pv", [128, 512], F32))
            ps_rb = e1(nc.psum_tensor("ps_rb", [128, 512], F32))
            ps_tp = e1(nc.psum_tensor("ps_tp", [128, 1024], BF16))

            def f(g):
                for dst, src in [(pT_buf[:, 0:8192], wq_io),
                                 (wkvg_sb[:], wkvg_io),
                                 (wo_sb[:], wo_io),
                                 (attnT_un[0:1, 0:1024], bq_io),
                                 (bkv_sb[:], bkv_io), (tri_sb[:], tri_io),
                                 (selz_sb[:], selz_io)]:
                    inc(g.dma_start(out=dst, in_=src), "cst", 16)
            GP(f)

            def f(sy):
                inc(sy.dma_start(out=bo_sb[:], in_=bo_io), "ld", 16)
                inc(sy.dma_start(out=cos_sb[:], in_=cos_io), "ld", 16)
                inc(sy.dma_start(out=sin_sb[:], in_=sin_io), "ld", 16)
            SY(f)
            w_cst0 = n("cst")
            w_ld = n("ld")

            def f(v):
                wt(v, "cst", w_cst0)
                wt(v, "ld", w_ld)
                raw = pT_buf[:, 0:8192]
                for g4, pab in enumerate([0, 0, 32, 32]):
                    off = g4 * 16
                    ov = wq_sb[:].rearrange("p (c h u) -> p c h u", c=8, h=16
                                            )[:, :, :, off:off + 16]
                    av = cos_sb[:].rearrange("p (h u) -> p h u", h=16
                                             )[:, :, off:off + 16].unsqueeze(1) \
                        .to_broadcast([128, 8, 16, 16])
                    bv = sin_sb[:].rearrange("p (h u) -> p h u", h=16
                                             )[:, :, off:off + 16].unsqueeze(1) \
                        .to_broadcast([128, 8, 16, 16])
                    e0 = raw.rearrange("p (c h u) -> p c h u", c=8, h=16
                                       )[:, :, :, pab:pab + 32:2]
                    e1v = raw.rearrange("p (c h u) -> p c h u", c=8, h=16
                                        )[:, :, :, pab + 1:pab + 32:2]
                    tmp = q_tile[:].rearrange("p (c h u) -> p c h u", c=8, h=16
                                              )[:, :, :, 0:16]
                    v.tensor_tensor(ov, e0, av, op=ALU.mult)
                    cs(v, v.tensor_tensor(tmp, e1v, bv, op=ALU.mult))
                    v.tensor_tensor(ov, ov, tmp, op=ALU.add)
                braw = attnT_un[0:1, 0:1024]
                for g4, pab in enumerate([0, 0, 32, 32]):
                    off = g4 * 16
                    ov = bq_sb[:].rearrange("p (h u) -> p h u", h=16
                                            )[:, :, off:off + 16]
                    av = cos_sb[0:1, :].rearrange("p (h u) -> p h u", h=16
                                                  )[:, :, off:off + 16]
                    bv = sin_sb[0:1, :].rearrange("p (h u) -> p h u", h=16
                                                  )[:, :, off:off + 16]
                    e0 = braw.rearrange("p (h u) -> p h u", h=16
                                        )[:, :, pab:pab + 32:2]
                    e1v = braw.rearrange("p (h u) -> p h u", h=16
                                         )[:, :, pab + 1:pab + 32:2]
                    tmp = q_tile[0:1, 0:256].rearrange("p (h u) -> p h u", h=16)
                    v.tensor_tensor(ov, e0, av, op=ALU.mult)
                    cs(v, v.tensor_tensor(tmp, e1v, bv, op=ALU.mult))
                    v.tensor_tensor(ov, ov, tmp, op=ALU.add)
                inc(v.engine_nop(), "dve")
            VE(f)
            wq_built = n("dve")
            set_res("pT_buf", "dve")
            set_res("q_tile0", "dve")
            w_cst = n("cst")

            dep = {}
            prev_rz = None

            for s in range(4):
                for t in range(16):
                    gt = s * 16 + t
                    bi = gt % 2
                    xsl = slice(bi * 1024, (bi + 1) * 1024)
                    d = dep.setdefault(gt, {})
                    p2 = dep.get(gt - 2 if serial < 2 else gt - 1)

                    def f(sy):
                        if p2 is not None:
                            sy.wait_ge(sems["dve"], p2["xn"])
                        inc(sy.dma_start(out=xbuf[:, xsl],
                                         in_=x_io[gt * 128:(gt + 1) * 128, :]),
                            "ldx%d" % bi, 16)
                    SY(f)
                    d["x"] = ("ldx%d" % bi, n("ldx%d" % bi))

                    def f(v):
                        v.wait_ge(sems[d["x"][0]], d["x"][1])
                        if p2 is not None:
                            v.wait_ge(sems["dve"], p2["xn"])
                        m0 = mrs[:, bi * 8:bi * 8 + 1]
                        s1v = mrs[:, bi * 8 + 2:bi * 8 + 3]
                        s2v = mrs[:, bi * 8 + 3:bi * 8 + 4]
                        ex2 = mrs[:, bi * 8 + 4:bi * 8 + 5]
                        var = mrs[:, bi * 8 + 5:bi * 8 + 6]
                        cs(v, v.tensor_tensor(sq_buf[:], xbuf[:, xsl],
                                              xbuf[:, xsl], op=ALU.mult))
                        cs(v, v.reduce_sum(s2v, sq_buf[:], axis=AX.X))
                        cs(v, v.reduce_sum(s1v, xbuf[:, xsl], axis=AX.X))
                        v.tensor_scalar(m0, s1v, 1.0 / 1024, None, op0=ALU.mult)
                        cs(v, v.tensor_scalar(ex2, s2v, 1.0 / 1024, None,
                                              op0=ALU.mult))
                        cs(v, v.tensor_tensor(var, m0, m0, op=ALU.mult))
                        inc(v.tensor_tensor(var, ex2, var, op=ALU.subtract), "dve")
                    VE(f)
                    d["var"] = n("dve")

                    def f(sc):
                        wt(sc, "dve", d["var"])
                        inc(sc.activation(mrs[:, bi * 8 + 6:bi * 8 + 7],
                                          mrs[:, bi * 8 + 5:bi * 8 + 6],
                                          AF.Sqrt, bias=eps_sb[:]), "act")
                    SC(f)
                    d["sqrt"] = n("act")

                    def f(v):
                        wt(v, "act", d["sqrt"])
                        use_res(v, "xn_bf%d" % bi)
                        cs(v, v.reciprocal(mrs[:, bi * 8 + 1:bi * 8 + 2],
                                           mrs[:, bi * 8 + 6:bi * 8 + 7]))
                        inc(v.tensor_scalar(xn_bf[:, xsl], xbuf[:, xsl],
                                            mrs[:, bi * 8:bi * 8 + 1],
                                            mrs[:, bi * 8 + 1:bi * 8 + 2],
                                            op0=ALU.subtract, op1=ALU.mult), "dve")
                    VE(f)
                    d["xn"] = n("dve")

                    if p1sub < 1:
                        if d_attn is not None:
                            def f(g):
                                wt(g, "dve", d["xn"])
                                inc(g.dma_start(
                                    out=d_attn[gt * 128:(gt + 1) * 128, :],
                                    in_=xn_bf[:, xsl]), "aro%d" % bi, 16)
                            GP(f)
                            set_res("xn_bf%d" % bi, "aro%d" % bi)
                        continue

                    def f(te):
                        wt(te, "cst", w_cst)
                        wt(te, "dve", d["xn"])
                        use_res(te, "ps_tp")
                        for j in range(8):
                            ins = te.transpose(ps_tp[:, j * 128:(j + 1) * 128],
                                               xn_bf[:, xsl][:, j * 128:(j + 1) * 128],
                                               ident_sb[:])
                        inc(ins, "pe")
                    TE(f)
                    d["xnT"] = n("pe")
                    set_res("xn_bf%d" % bi, "pe")

                    def f(v):
                        wt(v, "pe", d["xnT"])
                        use_res(v, "xnT_t%d" % bi)
                        inc(v.tensor_copy(xnT_t[:, bi * 1024:(bi + 1) * 1024],
                                          ps_tp[:]), "dve")
                    VE(f)
                    d["xnT_ev"] = n("dve")
                    set_res("ps_tp", "dve")

                    def f(g):
                        wt(g, "dve", d["xnT_ev"])
                        for j in range(8):
                            inc(g.dma_start(
                                out=xnT_sc[j, gt // 4, :,
                                           (gt % 4) * 128:(gt % 4) * 128 + 128],
                                in_=xnT_t[:, bi * 1024 + j * 128:
                                          bi * 1024 + (j + 1) * 128]),
                                "xns%d" % bi, 16)
                        if MOE_V1:
                            inc(g.dma_start(out=xnrow_sc[gt * 128:(gt + 1) * 128, :],
                                            in_=xn_bf[:, xsl]), "xnr%d" % bi, 16)
                    GP(f)
                    d["xnT_sto"] = ("xns%d" % bi, n("xns%d" % bi))
                    if MOE_V1:
                        add_res("xn_bf%d" % bi, "xnr%d" % bi)

                    def f(te):
                        wt(te, "ld", w_ld)
                        wt(te, "cst", w_cst)
                        te.wait_ge(sems["dve"], wq_built)
                        wt(te, "dve", d["xnT_ev"])
                        use_res(te, "ps_a0", "ps_a1", "ps_kv")
                        for grp in range(2):
                            for j in range(8):
                                te.matmul(ps_a[:, grp * 512:(grp + 1) * 512],
                                          xnT_t[:, bi * 1024 + j * 128:
                                                bi * 1024 + (j + 1) * 128],
                                          wq_sb[:, j * 1024 + grp * 512:
                                                j * 1024 + (grp + 1) * 512],
                                          start=(j == 0), stop=False)
                            inc(te.matmul(ps_a[:, grp * 512:(grp + 1) * 512],
                                          ones_r[:],
                                          bq_sb[:, grp * 512:(grp + 1) * 512],
                                          start=False, stop=True), "pe")
                        for j in range(8):
                            te.matmul(ps_kv[:, 0:136],
                                      xnT_t[:, bi * 1024 + j * 128:
                                            bi * 1024 + (j + 1) * 128],
                                      wkvg_sb[:, j * 136:(j + 1) * 136],
                                      start=(j == 0), stop=False)
                        inc(te.matmul(ps_kv[:, 0:136], ones_r[:], bkv_sb[:],
                                      start=False, stop=True), "pe")
                    TE(f)
                    d["proj"] = n("pe")
                    last_use["xnT_t%d" % bi] = [d["xnT_sto"],
                                                ("pe", d["proj"])]

                    def f(v):
                        wt(v, "pe", d["proj"])
                        use_res(v, "q_tile%d" % bi, "kv_t%d" % bi)
                        v.tensor_copy(q_tile[:, bi * 1024:(bi + 1) * 1024],
                                      ps_a[:])
                        cs(v, v.tensor_copy(kv_t[:, bi * 136:(bi + 1) * 136],
                                             ps_kv[:, 0:136]))
                        v.tensor_copy(l_all[:, gt * 8:(gt + 1) * 8],
                                      ps_kv[:, 128:136])
                        v.tensor_copy(k_seq[:, t * 64:(t + 1) * 64],
                                      kv_t[:, bi * 136:bi * 136 + 64])
                        v.tensor_copy(vaug_seq[:, t * 65:t * 65 + 64],
                                      kv_t[:, bi * 136 + 64:bi * 136 + 128])
                        inc(v.memset(vaug_seq[:, t * 65 + 64:t * 65 + 65], 1.0),
                            "dve")
                    VE(f)
                    d["proj_ev"] = n("dve")
                    set_res("ps_a0", "dve")
                    set_res("ps_a1", "dve")
                    set_res("ps_kv", "dve")
                    set_res("kv_t%d" % bi, "dve")

                    def f(te):
                        wt(te, "dve", d["proj_ev"])
                        use_res(te, "ps_tp")
                        for j in range(8):
                            ins = te.transpose(ps_tp[:, j * 128:(j + 1) * 128],
                                               q_tile[:, bi * 1024 + j * 128:
                                                      bi * 1024 + (j + 1) * 128],
                                               ident_sb[:])
                        inc(ins, "pe")
                    TE(f)
                    d["qT"] = n("pe")
                    set_res("q_tile%d" % bi, "pe")

                    def f(v):
                        wt(v, "pe", d["qT"])
                        for j in range(8):
                            ins = v.tensor_copy(qT_seq[:, j * 2048 + t * 128:
                                                       j * 2048 + (t + 1) * 128],
                                                ps_tp[:, j * 128:(j + 1) * 128])
                        inc(ins, "dve")
                    VE(f)
                    d["qT_ev"] = n("dve")
                    set_res("ps_tp", "dve")
                    if serial >= 1:
                        join_all()

                if p1sub < 2:
                    continue

                kr_done = n("dve")
                bp_k = n("pe")
                bd_k = n("dve")

                def f(te):
                    wt(te, "dve", kr_done)
                    use_res(te, "ps_tp")
                    for t2 in range(16):
                        if t2 >= 2:
                            te.wait_ge(sems["dve"], bd_k + t2 - 1)
                        inc(te.transpose(ps_tp[:, (t2 % 2) * 512:
                                               (t2 % 2) * 512 + 128],
                                         k_seq[:, t2 * 64:t2 * 64 + 128],
                                         ident_sb[:]), "pe")
                TE(f)

                def f(v):
                    for t2 in range(16):
                        v.wait_ge(sems["pe"], bp_k + t2 + 1)
                        inc(v.tensor_copy(kT_seq[0:64, t2 * 128:(t2 + 1) * 128],
                                          ps_tp[0:64, (t2 % 2) * 512:
                                                (t2 % 2) * 512 + 128]), "dve")
                VE(f)
                assert n("pe") == bp_k + 16
                kT_ev = n("dve")
                set_res("ps_tp", "dve")

                def f(sy):
                    wt(sy, "dve", kT_ev)
                    inc(sy.dma_start(out=kT_seq[64:128, :],
                                     in_=kT_seq[0:64, :]), "ktd", 16)
                SY(f)
                kT_done = ("ktd", n("ktd"))

                if MOE_V1 and s == 3:
                    def f(g):
                        wt(g, "xnr0")
                        wt(g, "xnr1")
                        inc(g.collective_compute(
                            "AllGather", ALU.bypass,
                            replica_groups=[list(range(NCORES))],
                            ins=[xnrow_sc.opt()], outs=[xn_all.opt()]), "ccx")
                    GP(f)

                if p1sub < 3:
                    continue
                for h in range(NH):
                    for g4 in range(4):
                        nchunk = 4 * g4 + 4
                        prev_exp = n("act")
                        prev_pv_d = last_use.get("pT_buf", ("dve", 0))
                        base_pe = n("pe")

                        def f(te):
                            te.wait_ge(sems[kT_done[0]], kT_done[1])
                            wt(te, "cst", w_cst)
                            use_res(te, "ps_s0", "ps_s1")
                            qb = (h % 2) * 64
                            for c in range(nchunk):
                                if c >= 2:
                                    te.wait_ge(sems["act"], prev_exp + c - 1)
                                bank = c % 2
                                diag = c >= 4 * g4
                                ins = te.matmul(
                                    ps_s[:, bank * 512:(bank + 1) * 512],
                                    kT_seq[qb:qb + 64, c * 128:(c + 1) * 128],
                                    qT_seq[qb:qb + 64,
                                           (h // 2) * 2048 + g4 * 512:
                                           (h // 2) * 2048 + (g4 + 1) * 512],
                                    start=True, stop=not diag)
                                if diag:
                                    j = c - 4 * g4
                                    ins = te.matmul(
                                        ps_s[:, bank * 512:(bank + 1) * 512],
                                        ident_sb[:],
                                        tri_sb[:, j * 512:(j + 1) * 512],
                                        start=False, stop=True)
                                inc(ins, "pe")
                        TE(f)
                        assert n("pe") == base_pe + nchunk

                        def f(sc):
                            use_res(sc, "pT_buf")
                            for c in range(nchunk):
                                sc.wait_ge(sems["pe"], base_pe + c + 1)
                                inc(sc.activation(pT_buf[:, c * 512:(c + 1) * 512],
                                                  ps_s[:, (c % 2) * 512:
                                                       (c % 2) * 512 + 512],
                                                  AF.Exp, scale=0.125), "act")
                        SC(f)
                        exp_done = n("act")
                        set_res("ps_s0", "act")
                        set_res("ps_s1", "act")

                        def f(te):
                            use_res(te, "ps_pv", "ps_pv2")
                            for c in range(nchunk):
                                te.wait_ge(sems["act"],
                                           exp_done - nchunk + c + 1)
                                ins = te.matmul(ps_pv[0:65, :],
                                                vaug_seq[:, c * 65:(c + 1) * 65],
                                                pT_buf[:, c * 512:(c + 1) * 512],
                                                start=(c == 0),
                                                stop=(c == nchunk - 1))
                            inc(ins, "pe")
                        TE(f)
                        pv_done = n("pe")
                        set_res("pT_buf", "pe")

                        sbi = (h * 4 + g4) % 2

                        def f(v):
                            wt(v, "pe", pv_done)
                            use_res(v, "stag%d" % sbi)
                            inc(v.tensor_copy(
                                stag_sb[0:65, sbi * 512:sbi * 512 + 512],
                                ps_pv[0:65, :]), "dve")
                        VE(f)
                        stag_ev = n("dve")
                        set_res("ps_pv", "dve")

                        def f(sy):
                            wt(sy, "dve", stag_ev)
                            if prev_rz is not None and h == 0 and g4 == 0:
                                sy.wait_ge(sems["dve"], prev_rz)
                            inc(sy.dma_start(
                                out=attnT_un[(h % 2) * 64:(h % 2) * 64 + 64,
                                             (h // 2) * 2048 + g4 * 512:
                                             (h // 2) * 2048 + (g4 + 1) * 512],
                                in_=stag_sb[0:64, sbi * 512:sbi * 512 + 512]),
                                "stg%d" % sbi, 16)
                            inc(sy.dma_start(
                                out=z_seq[h:h + 1, g4 * 512:(g4 + 1) * 512],
                                in_=stag_sb[64:65, sbi * 512:sbi * 512 + 512]),
                                "stg%d" % sbi, 16)
                        SY(f)
                        set_res("stag%d" % sbi, "stg%d" % sbi)

                if p1sub < 4:
                    continue

                def f(v):
                    wt(v, "act")
                    wt(v, "dve")
                    wt(v, "stg0")
                    wt(v, "stg1")
                    with nc.allow_low_precision(
                            reason="softmax denom recip on bf16 tiles; "
                                   "error ~0.4%% of a small attn term"):
                        inc(v.reciprocal(rz_seq[:], z_seq[:]), "dve")
                VE(f)
                rz_done = n("dve")
                prev_rz = rz_done

                for g4 in range(4):
                    for cch in range(8):
                        bi2 = cch % 2

                        def f(te):
                            wt(te, "dve", rz_done)
                            use_res(te, "ps_rb")
                            inc(te.matmul(ps_rb[:],
                                          selz_sb[:, cch * 128:(cch + 1) * 128],
                                          rz_seq[:, g4 * 512:(g4 + 1) * 512],
                                          start=True, stop=True), "pe")
                        TE(f)
                        rbmm = n("pe")

                        def f(v):
                            wt(v, "pe", rbmm)
                            use_res(v, "rb%d" % bi2)
                            cs(v, v.tensor_copy(rb_sb[:, bi2 * 512:(bi2 + 1) * 512],
                                                ps_rb[:]))
                            inc(v.tensor_tensor(
                                anorm_g[:, cch * 512:(cch + 1) * 512],
                                attnT_un[:, cch * 2048 + g4 * 512:
                                         cch * 2048 + (g4 + 1) * 512],
                                rb_sb[:, bi2 * 512:(bi2 + 1) * 512],
                                op=ALU.mult), "dve")
                        VE(f)
                        set_res("ps_rb", "dve")
                        set_res("rb%d" % bi2, "dve")
                    anorm_done = n("dve")

                    for co in range(8):
                        ob = co % 2

                        def f(te):
                            wt(te, "dve", anorm_done)
                            use_res(te, "ps_a%d" % ob)
                            for cch in range(8):
                                ins = te.matmul(
                                    ps_a[:, ob * 512:ob * 512 + 512],
                                    wo_sb[:, cch * 1024 + co * 128:
                                          cch * 1024 + co * 128 + 128],
                                    anorm_g[:, cch * 512:(cch + 1) * 512],
                                    start=(cch == 0), stop=(cch == 7))
                            inc(ins, "pe")
                        TE(f)
                        omm = n("pe")

                        def f(v):
                            wt(v, "pe", omm)
                            wt(v, "ld", w_ld)
                            inc(v.tensor_scalar(aoutT_g[:, co * 512:(co + 1) * 512],
                                                ps_a[:, ob * 512:ob * 512 + 512],
                                                bo_sb[:, co:co + 1], None,
                                                op0=ALU.add), "dve")
                        VE(f)
                        set_res("ps_a%d" % ob, "dve")
                    aout_done = n("dve")

                    for tq in range(4):
                        gt2 = s * 16 + g4 * 4 + tq
                        bi3 = tq % 2

                        def f(te):
                            wt(te, "dve", aout_done)
                            use_res(te, "ps_tp")
                            for co in range(8):
                                ins = te.transpose(
                                    ps_tp[:, co * 128:(co + 1) * 128],
                                    aoutT_g[:, co * 512 + tq * 128:
                                            co * 512 + tq * 128 + 128],
                                    ident_sb[:])
                            inc(ins, "pe")
                        TE(f)
                        rtp = n("pe")

                        def f(v):
                            wt(v, "pe", rtp)
                            use_res(v, "arow%d" % bi3)
                            inc(v.tensor_copy(arow_sb[:, bi3 * 1024:
                                                      (bi3 + 1) * 1024],
                                              ps_tp[:]), "dve")
                        VE(f)
                        rev = n("dve")
                        set_res("ps_tp", "dve")

                        def f(g):
                            wt(g, "dve", rev)
                            inc(g.dma_start(out=attn_sc[gt2 * 128:
                                                        (gt2 + 1) * 128, :],
                                            in_=arow_sb[:, bi3 * 1024:
                                                        (bi3 + 1) * 1024]),
                                "aro%d" % bi3, 16)
                        GP(f)
                        set_res("arow%d" % bi3, "aro%d" % bi3)

        join_all()
        last_use.clear()

        with ExitStack() as s2:
            if phases >= 2:
                e2 = s2.enter_context
                v1_sb = e2(nc.sbuf_tensor("v1r", [128, 64], F32))
                v2_sb = e2(nc.sbuf_tensor("v2r", [128, 64], F32))
                d3_sb = e2(nc.sbuf_tensor("d3r", [128, 512], F32))
                m3_sb = e2(nc.sbuf_tensor("m3r", [128, 512], F32))
                l2_sb = e2(nc.sbuf_tensor("l2r", [128, 512], F32))
                sg_sb = e2(nc.sbuf_tensor("sgr", [128, 64], F32))
                g1_sb = e2(nc.sbuf_tensor("g1r", [128, 64], F32))
                g2_sb = e2(nc.sbuf_tensor("g2r", [128, 64], F32))
                msk_sb = e2(nc.sbuf_tensor("mskr", [128, 16 * 64], F32))
                pos_sb = e2(nc.sbuf_tensor("posr", [128, 16 * 64], F32))
                p1_sb = e2(nc.sbuf_tensor("p1r", [128, 64], F32))
                p2_sb = e2(nc.sbuf_tensor("p2r", [128, 64], F32))
                cs_sb = e2(nc.sbuf_tensor("csr", [1, 16 * 64], F32))
                cnt_sb = e2(nc.sbuf_tensor("cntr", [1, 16], F32))
                scn_sb = e2(nc.sbuf_tensor("scnr", [1, 16 * 64], F32))
                zer_sb = e2(nc.sbuf_tensor("zerr", [1, 64], F32))
                agi_sb = e2(nc.sbuf_tensor("agir", [16, 1], F32))
                ago_sb = e2(nc.sbuf_tensor("agor", [128, 1], F32))
                base_sb = e2(nc.sbuf_tensor("baser", [16, 1], F32))
                baser_sb = e2(nc.sbuf_tensor("baserr", [1, 16], F32))
                ce_sb = e2(nc.sbuf_tensor("cer", [128, 128], BF16))
                ceT_sb = e2(nc.sbuf_tensor("ceTr", [64, 128], BF16))
                sc_sb = e2(nc.sbuf_tensor("scr", [128, 64], F32))
                vv_sb = e2(nc.sbuf_tensor("vvr", [128, 2 * 64], F32))
                ps_r = e2(nc.psum_tensor("ps_r", [128, 64], F32))
                ps_row = e2(nc.psum_tensor("ps_row", [1, 512], F32))
                ps_16 = e2(nc.psum_tensor("ps_16", [16, 64], F32))
                ps_ct = e2(nc.psum_tensor("ps_ct", [128, 128], BF16))

                def f(v):
                    l3 = l_all[:].rearrange("p (t e) -> p t e", e=8)
                    cs(v, v.tensor_reduce(v1_sb[:].unsqueeze(2), l3, axis=AX.X,
                                          op=ALU.max))
                    inc(v.tensor_tensor(d3_sb[:].rearrange("p (t e) -> p t e", e=8),
                                        l3, v1_sb[:].unsqueeze(2)
                                        .to_broadcast([128, 64, 8]),
                                        op=ALU.subtract), "dve")
                VE(f)

                def f(sc):
                    wt(sc, "dve")
                    inc(sc.sign(m3_sb[:], d3_sb[:]), "act")
                SC(f)

                def f(v):
                    wt(v, "act")
                    cs(v, v.tensor_scalar(m3_sb[:], m3_sb[:], 1.0, None,
                                          op0=ALU.add))
                    cs(v, v.tensor_tensor(d3_sb[:].rearrange("p (t e) -> p t e", e=8),
                                          m3_sb[:].rearrange("p (t e) -> p t e", e=8),
                                          iotar_sb[:].unsqueeze(1)
                                          .to_broadcast([128, 64, 8]), op=ALU.mult))
                    cs(v, v.tensor_reduce(i1_sb[:].unsqueeze(2),
                                          d3_sb[:].rearrange("p (t e) -> p t e", e=8),
                                          axis=AX.X, op=ALU.max))
                    cs(v, v.tensor_scalar(i1_sb[:], i1_sb[:], -1.0, 8.0,
                                          op0=ALU.mult, op1=ALU.add))
                    cs(v, v.tensor_scalar(d3_sb[:, 0:8], iotar_sb[:], -1.0, 8.0,
                                          op0=ALU.mult, op1=ALU.add))
                    inc(v.tensor_tensor(l2_sb[:].rearrange("p (t e) -> p t e", e=8),
                                        d3_sb[:, 0:8].unsqueeze(1)
                                        .to_broadcast([128, 64, 8]),
                                        i1_sb[:].unsqueeze(2)
                                        .to_broadcast([128, 64, 8]),
                                        op=ALU.subtract), "dve")
                VE(f)

                def f(sc):
                    wt(sc, "dve")
                    inc(sc.sign(m3_sb[:], l2_sb[:]), "act")
                SC(f)

                def f(v):
                    wt(v, "act")
                    cs(v, v.tensor_tensor(m3_sb[:], m3_sb[:], m3_sb[:], op=ALU.mult))
                    cs(v, v.tensor_scalar(m3_sb[:], m3_sb[:], -1.0, 1.0,
                                          op0=ALU.mult, op1=ALU.add))
                    cs(v, v.scalar_tensor_tensor(l2_sb[:], m3_sb[:], -1e9, l_all[:],
                                                 op0=ALU.mult, op1=ALU.add))
                    cs(v, v.tensor_reduce(v2_sb[:].unsqueeze(2),
                                          l2_sb[:].rearrange("p (t e) -> p t e", e=8),
                                          axis=AX.X, op=ALU.max))
                    inc(v.tensor_tensor(d3_sb[:].rearrange("p (t e) -> p t e", e=8),
                                        l2_sb[:].rearrange("p (t e) -> p t e", e=8),
                                        v2_sb[:].unsqueeze(2)
                                        .to_broadcast([128, 64, 8]),
                                        op=ALU.subtract), "dve")
                VE(f)

                def f(sc):
                    wt(sc, "dve")
                    inc(sc.sign(m3_sb[:], d3_sb[:]), "act")
                SC(f)

                def f(v):
                    wt(v, "act")
                    cs(v, v.tensor_scalar(m3_sb[:], m3_sb[:], 1.0, None,
                                          op0=ALU.add))
                    cs(v, v.tensor_tensor(d3_sb[:].rearrange("p (t e) -> p t e", e=8),
                                          m3_sb[:].rearrange("p (t e) -> p t e", e=8),
                                          iotar_sb[:].unsqueeze(1)
                                          .to_broadcast([128, 64, 8]), op=ALU.mult))
                    cs(v, v.tensor_reduce(i2_sb[:].unsqueeze(2),
                                          d3_sb[:].rearrange("p (t e) -> p t e", e=8),
                                          axis=AX.X, op=ALU.max))
                    cs(v, v.tensor_scalar(i2_sb[:], i2_sb[:], -1.0, 8.0,
                                          op0=ALU.mult, op1=ALU.add))
                    inc(v.tensor_tensor(sg_sb[:], v1_sb[:], v2_sb[:],
                                        op=ALU.subtract), "dve")
                VE(f)

                def f(sc):
                    wt(sc, "dve")
                    inc(sc.activation(g1_sb[:], sg_sb[:], AF.Sigmoid), "act")
                SC(f)

                def f(v):
                    wt(v, "act")
                    inc(v.tensor_scalar(g2_sb[:], g1_sb[:], -1.0, 1.0,
                                        op0=ALU.mult, op1=ALU.add), "dve")
                VE(f)

                for k in range(2):
                    isrc = i1_sb if k == 0 else i2_sb
                    for ee in range(E):
                        ix = k * 8 + ee

                        def f(v):
                            inc(v.tensor_scalar(sc_sb[:], isrc[:], float(ee), None,
                                                op0=ALU.subtract), "dve")
                        VE(f)

                        def f(sc):
                            wt(sc, "dve")
                            inc(sc.sign(msk_sb[:, ix * 64:(ix + 1) * 64], sc_sb[:]),
                                "act")
                        SC(f)

                        def f(v):
                            wt(v, "act")
                            mv = msk_sb[:, ix * 64:(ix + 1) * 64]
                            cs(v, v.tensor_tensor(mv, mv, mv, op=ALU.mult))
                            inc(v.tensor_scalar(mv, mv, -1.0, 1.0,
                                                op0=ALU.mult, op1=ALU.add), "dve")
                        VE(f)
                        msk_done = n("dve")

                        def f(te):
                            wt(te, "dve", msk_done)
                            use_res(te, "ps_row")
                            inc(te.matmul(ps_row[:, 0:64], ones_cf[:],
                                          msk_sb[:, ix * 64:(ix + 1) * 64],
                                          start=True, stop=True), "pe")
                        TE(f)
                        cs_mm = n("pe")

                        def f(v):
                            wt(v, "pe", cs_mm)
                            cs(v, v.tensor_copy(cs_sb[:, ix * 64:(ix + 1) * 64],
                                                ps_row[:, 0:64]))
                            inc(v.reduce_sum(cnt_sb[:, ix:ix + 1],
                                             cs_sb[:, ix * 64:(ix + 1) * 64],
                                             axis=AX.X), "dve")
                        VE(f)
                        set_res("ps_row", "dve")
                cnts_done = n("dve")

                def f(te):
                    wt(te, "dve", cnts_done)
                    inc(te.matmul(ps_16[0:16, 0:1], cnt_sb[:], ones_rf[:, 0:1],
                                  start=True, stop=True), "pe")
                TE(f)
                tr_cnt = n("pe")

                def f(v):
                    wt(v, "pe", tr_cnt)
                    v.memset(zer_sb[:], 0.0)
                    inc(v.tensor_copy(agi_sb[:], ps_16[0:16, 0:1]), "dve")
                VE(f)
                set_res("ps_16", "dve")

                def f(g):
                    wt(g, "dve")
                    inc(g.dma_start(out=ag_in, in_=agi_sb[:]), "sto", 16)
                    wt(g, "sto")
                    inc(g.collective_compute(
                        "AllGather", ALU.bypass,
                        replica_groups=[list(range(NCORES))],
                        ins=[ag_in.opt()], outs=[ag_out.opt()]), "cc")
                    wt(g, "cc")
                    inc(g.dma_start(out=ago_sb[:], in_=ag_out), "agl", 16)
                GP(f)
                ag_done = n("agl")

                def f(te):
                    wt(te, "agl", ag_done)
                    use_res(te, "ps_16")
                    inc(te.matmul(ps_16[0:16, 0:1], selb_sb[:], ago_sb[:],
                                  start=True, stop=True), "pe")
                TE(f)
                base_mm = n("pe")

                def f(v):
                    wt(v, "pe", base_mm)
                    inc(v.tensor_copy(base_sb[:], ps_16[0:16, 0:1]), "dve")
                VE(f)
                base_done = n("dve")

                def f(g):
                    wt(g, "dve", base_done)
                    inc(g.dma_start(out=base_bounce, in_=base_sb[:]), "sto", 16)
                    wt(g, "sto")
                    inc(g.dma_start(out=baser_sb[:],
                                    in_=base_bounce.rearrange("a b -> (a b)")
                                    .unsqueeze(0)), "agl", 16)
                GP(f)
                baser_done = n("agl")

                for k in range(2):
                    for ee in range(E):
                        ix = k * 8 + ee

                        def f(v):
                            wt(v, "agl", baser_done)
                            cs(v, v.tensor_tensor_scan(
                                scn_sb[:, ix * 64:(ix + 1) * 64],
                                cs_sb[:, ix * 64:(ix + 1) * 64],
                                zer_sb[:],
                                baser_sb[0:1, ix:ix + 1],
                                op0=ALU.add, op1=ALU.add))
                            inc(v.tensor_tensor(scn_sb[:, ix * 64:(ix + 1) * 64],
                                                scn_sb[:, ix * 64:(ix + 1) * 64],
                                                cs_sb[:, ix * 64:(ix + 1) * 64],
                                                op=ALU.subtract), "dve")
                        VE(f)
                        scn_done = n("dve")

                        def f(te):
                            wt(te, "dve", scn_done)
                            use_res(te, "ps_r")
                            te.matmul(ps_r[:], ltri_sb[:],
                                      msk_sb[:, ix * 64:(ix + 1) * 64],
                                      start=True, stop=False)
                            inc(te.matmul(ps_r[:], ones_rf[:],
                                          scn_sb[:, ix * 64:(ix + 1) * 64],
                                          start=False, stop=True), "pe")
                        TE(f)
                        pos_mm = n("pe")

                        def f(v):
                            wt(v, "pe", pos_mm)
                            inc(v.tensor_copy(pos_sb[:, ix * 64:(ix + 1) * 64],
                                              ps_r[:]), "dve")
                        VE(f)
                        set_res("ps_r", "dve")

                def f(v):
                    for k, pk in enumerate([p1_sb, p2_sb]):
                        cs(v, v.memset(pk[:], 0.0))
                        for ee in range(E):
                            ix = k * 8 + ee
                            cs(v, v.tensor_tensor(
                                sc_sb[:],
                                msk_sb[:, ix * 64:(ix + 1) * 64],
                                pos_sb[:, ix * 64:(ix + 1) * 64],
                                op=ALU.mult))
                            cs(v, v.tensor_tensor(pk[:], pk[:], sc_sb[:],
                                                  op=ALU.add))
                    v.tensor_scalar(vv_sb[:, 0:64], p1_sb[:], CAP - 0.5, None,
                                    op0=ALU.subtract)
                    inc(v.tensor_scalar(vv_sb[:, 64:128], p2_sb[:], CAP - 0.5, None,
                                        op0=ALU.subtract), "dve")
                VE(f)

                def f(sc):
                    wt(sc, "dve")
                    inc(sc.sign(d3_sb[:, 0:128], vv_sb[:]), "act")
                SC(f)

                def f(v):
                    wt(v, "act")
                    cs(v, v.tensor_scalar(d3_sb[:, 0:128], d3_sb[:, 0:128],
                                          -0.5, 0.5, op0=ALU.mult, op1=ALU.add))
                    v.tensor_tensor(gv1_sb[:], g1_sb[:], d3_sb[:, 0:64], op=ALU.mult)
                    inc(v.tensor_tensor(gv2_sb[:], g2_sb[:], d3_sb[:, 64:128],
                                        op=ALU.mult), "dve")
                VE(f)
                gv_done = n("dve")

                if MOE_V1:
                    spk = msk_sb[:, 0:128]
                    dstp = msk_sb[:, 128:256]
                    pr1 = msk_sb[:, 256:384]
                    pr2 = msk_sb[:, 384:512]
                    dsti = e2(nc.sbuf_tensor("dsti", [128, 128],
                                             mybir.dt.int32))
                    zbig = e2(nc.sbuf_tensor("zbig", [128, 4096], F32))

                    def f(v):
                        wt(v, "dve", gv_done)
                        for k2, (ik, pk, vo) in enumerate(
                                [(i1_sb, p1_sb, 0), (i2_sb, p2_sb, 64)]):
                            cs(v, v.tensor_scalar_min(spk[:, vo:vo + 64],
                                                      pk[:], 16383.0))
                            cs(v, v.tensor_scalar(sc_sb[:], ik[:],
                                                  float(ESLOT), None,
                                                  op0=ALU.mult))
                            cs(v, v.tensor_tensor(spk[:, vo:vo + 64],
                                                  spk[:, vo:vo + 64],
                                                  sc_sb[:], op=ALU.add))
                            cs(v, v.tensor_scalar(sc_sb[:],
                                                  d3_sb[:, vo:vo + 64],
                                                  -131072.0, 131072.0,
                                                  op0=ALU.mult, op1=ALU.add))
                            cs(v, v.tensor_tensor(dstp[:, vo:vo + 64],
                                                  spk[:, vo:vo + 64],
                                                  sc_sb[:], op=ALU.add))
                        cs(v, v.tensor_scalar(sc_sb[:], gidx_sb[:], 1.0, None,
                                              op0=ALU.add))
                        for prk, gvk in [(pr1, gv1_sb), (pr2, gv2_sb)]:
                            cs(v, v.tensor_copy(
                                prk.rearrange("p (t c) -> p t c", c=2)[:, :, 0],
                                sc_sb[:]))
                            cs(v, v.tensor_copy(
                                prk.rearrange("p (t c) -> p t c", c=2)[:, :, 1],
                                gvk[:]))
                        v.memset(zbig[:], 0.0)
                        inc(v.engine_nop(), "dve")
                    VE(f)
                    v1r_done = n("dve")

                    def f(g):
                        wt(g, "dve", v1r_done)
                        inc(g.dma_start(out=dsti[:], in_=dstp), "bsc", 16)
                        inc(g.dma_start(
                            out=buck_loc.rearrange("(p r) c -> p (r c)", p=128),
                            in_=zbig[:]), "bsc", 16)
                        wt(g, "bsc")
                        for k2, prk in [(0, pr1), (1, pr2)]:
                            for T2 in range(64):
                                col = k2 * 64 + T2
                                inc(g.indirect_dma_start(
                                    buck_loc,
                                    IndirectOffsetOnAxis(dsti[:, col:col + 1], 0),
                                    prk[:, 2 * T2:2 * T2 + 2], None),
                                    "bsc", 16)
                        wt(g, "bsc")
                        inc(g.collective_compute(
                            "AllReduce", ALU.add,
                            replica_groups=[list(range(NCORES))],
                            ins=[buck_loc[0:8 * ESLOT, :].opt()],
                            outs=[buck_all.opt()]), "ccb")
                    GP(f)

                for ee in (range(E) if not MOE_V1 else []):
                    def f(v):
                        wt(v, "dve", gv_done)
                        use_res(v, "ce")
                        v.tensor_tensor(sc_sb[:], gv1_sb[:],
                                        msk_sb[:, ee * 64:(ee + 1) * 64],
                                        op=ALU.mult)
                        cs(v, v.tensor_tensor(d3_sb[:, 0:64], gv2_sb[:],
                                              msk_sb[:, (8 + ee) * 64:
                                                     (9 + ee) * 64],
                                              op=ALU.mult))
                        cs(v, v.tensor_tensor(sc_sb[:], sc_sb[:], d3_sb[:, 0:64],
                                              op=ALU.add))
                        if ee == 0:
                            v.memset(ce_sb[:, 64:128], 0.0)
                        inc(v.tensor_copy(ce_sb[:, 0:64], sc_sb[:]), "dve")
                    VE(f)
                    ce_done = n("dve")

                    def f(te):
                        wt(te, "dve", ce_done)
                        use_res(te, "ps_ct")
                        inc(te.transpose(ps_ct[:], ce_sb[:], ident_sb[:]), "pe")
                    TE(f)
                    ct_mm = n("pe")

                    def f(v):
                        wt(v, "pe", ct_mm)
                        use_res(v, "ceT")
                        inc(v.tensor_copy(ceT_sb[:], ps_ct[0:64, :]), "dve")
                    VE(f)
                    ctev = n("dve")
                    set_res("ps_ct", "dve")
                    set_res("ce", "dve")

                    def f(g):
                        wt(g, "dve", ctev)
                        inc(g.dma_start(out=crow_sc[ee], in_=ceT_sb[:]), "sto", 16)
                    GP(f)
                    set_res("ceT", "sto")

                if d_lall is not None:
                    def f(g):
                        wt(g, "dve")
                        inc(g.dma_start(out=d_lall, in_=l_all[:]), "outs", 16)
                    GP(f)
                if d_route is not None:
                    def f(g):
                        wt(g, "dve")
                        for j, src in enumerate([i1_sb, i2_sb, gv1_sb, gv2_sb,
                                                 p1_sb, p2_sb, v1_sb, v2_sb]):
                            inc(g.dma_start(out=d_route[:, j * 64:(j + 1) * 64],
                                            in_=src[:]), "outs", 16)
                    GP(f)
                if d_cnt is not None:
                    def f(g):
                        wt(g, "agl")
                        inc(g.dma_start(out=d_cnt, in_=ago_sb[:]), "outs", 16)
                    GP(f)

        join_all()
        last_use.clear()

        with ExitStack() as s3:
            if phases >= 3 and MOE_V1:
                e3 = s3.enter_context
                I32 = mybir.dt.int32
                w1_sb = e3(nc.sbuf_tensor("w1_sb", [128, 8 * HDN], BF16))
                w2_sb = e3(nc.sbuf_tensor("w2_sb", [128, 32 * 1024], BF16))
                xrow = e3(nc.sbuf_tensor("xrow", [128, 2 * 1024], BF16))
                xg = e3(nc.sbuf_tensor("xg", [128, 8 * 512], BF16))
                hT_g = e3(nc.sbuf_tensor("hT_g", [128, 32 * 512], BF16))
                yT_sb = e3(nc.sbuf_tensor("yT_sb", [128, 8 * 512], BF16))
                yr_sb = e3(nc.sbuf_tensor("yr_sb", [128, 4 * 1024], BF16))
                prg = e3(nc.sbuf_tensor("prg", [128, 16], F32))
                tokf = e3(nc.sbuf_tensor("tokf", [128, 8], F32))
                toki = e3(nc.sbuf_tensor("toki", [128, 8], I32))
                gat = e3(nc.sbuf_tensor("gat", [128, 8], F32))
                dstf = e3(nc.sbuf_tensor("dstf3", [128, 8], F32))
                b1r_sb = e3(nc.sbuf_tensor("b1r_sb", [1, HDN], BF16))
                b2r_sb = e3(nc.sbuf_tensor("b2r_sb", [1, D], BF16))
                ones5 = e3(nc.sbuf_tensor("ones5", [1, 512], BF16))
                dsti = e3(nc.sbuf_tensor("dsti3", [128, 8], I32))
                ps_h = e3(nc.psum_tensor("ps_h", [128, 1024], F32))
                ps_y = e3(nc.psum_tensor("ps_y", [128, 1024], F32))
                ps_tx = e3(nc.psum_tensor("ps_tx", [128, 2048], BF16))
                ps_ty = e3(nc.psum_tensor("ps_ty", [128, 2048], BF16))

                def f(sy):
                    wt(sy, "wbf")
                    inc(sy.dma_start(out=w1_sb[:], in_=w1bf), "wld", 16)
                SY(f)

                def f(sc):
                    wt(sc, "wbf")
                    inc(sc.dma_start(out=w2_sb[:], in_=w2bf), "wld", 16)
                SC(f)

                def f(g):
                    inc(g.dma_start(out=b1r_sb[:], in_=b1r_io), "wbf", 16)
                    inc(g.dma_start(out=b2r_sb[:], in_=b2r_io), "wbf", 16)
                GP(f)
                w_done = n("wld")
                wbf_done = n("wbf")

                def f(v):
                    inc(v.memset(ones5[:], 1.0), "dve")
                VE(f)
                o5_done = n("dve")

                pend_scat = []

                def flush_scat():
                    for fn, s2, pb2 in pend_scat:
                        GP(fn)
                        set_res("yr%d" % s2, "ym%d" % (s2 % 2))
                    if pend_scat:
                        pb2 = pend_scat[0][2]
                        add_res("toki%d" % pb2, "ym0")
                        add_res("toki%d" % pb2, "ym1")
                    del pend_scat[:]

                for gg in range(32):
                    pb = gg % 2
                    q0 = 4 * gg

                    def f(g):
                        use_res(g, "prg%d" % pb)
                        for s in range(4):
                            inc(g.indirect_dma_start(
                                prg[:, pb * 8 + 2 * s:pb * 8 + 2 * s + 2],
                                None, buck_all,
                                IndirectOffsetOnAxis(
                                    ebase_sb[:, q0 + s:q0 + s + 1], 0)),
                                "pg%d" % pb, 16)
                    GP(f)
                    pg_done = n("pg%d" % pb)

                    def f(v):
                        wt(v, "pg%d" % pb, pg_done)
                        use_res(v, "tok%d" % pb)
                        pv = prg[:, pb * 8:(pb + 1) * 8].rearrange(
                            "p (s c) -> p s c", c=2)
                        dh = dstf[:, pb * 4:(pb + 1) * 4]
                        th = tokf[:, pb * 4:(pb + 1) * 4]
                        cs(v, v.tensor_scalar(th, pv[:, :, 0], -1.0, 0.0,
                                              op0=ALU.add, op1=ALU.max))
                        cs(v, v.tensor_scalar_min(dh, pv[:, :, 0], 1.0))
                        cs(v, v.tensor_scalar(dh, dh, -65536.0, 65536.0,
                                              op0=ALU.mult, op1=ALU.add))
                        cs(v, v.tensor_tensor(dh, dh, th, op=ALU.add))
                        inc(v.tensor_copy(gat[:, pb * 4:(pb + 1) * 4],
                                          pv[:, :, 1]), "dve")
                    VE(f)
                    tg_done = n("dve")
                    set_res("prg%d" % pb, "dve")

                    def f(g):
                        wt(g, "dve", tg_done)
                        use_res(g, "toki%d" % pb)
                        inc(g.dma_start(out=toki[:, pb * 4:(pb + 1) * 4],
                                        in_=tokf[:, pb * 4:(pb + 1) * 4]),
                            "ct%d" % pb, 16)
                        inc(g.dma_start(out=dsti[:, pb * 4:(pb + 1) * 4],
                                        in_=dstf[:, pb * 4:(pb + 1) * 4]),
                            "ct%d" % pb, 16)
                    GP(f)
                    ct_done = n("ct%d" % pb)
                    set_res("tok%d" % pb, "ct%d" % pb)

                    for s2 in range(2):
                        xsem = "xga" if s2 == 0 else "xgb"

                        def f(g, s2=s2, xsem=xsem, ct_done=ct_done):
                            wt(g, "ct%d" % pb, ct_done)
                            use_res(g, "xrow")
                            for u in range(2):
                                s = 2 * s2 + u
                                inc(g.indirect_dma_start(
                                    xrow[:, u * 1024:(u + 1) * 1024], None,
                                    xn_all,
                                    IndirectOffsetOnAxis(
                                        toki[:, pb * 4 + s:pb * 4 + s + 1],
                                        0)), xsem, 16)
                        GP(f)
                        xr_ld = n(xsem)
                        if s2 == 0:
                            set_res("toki%d" % pb, xsem)
                        else:
                            add_res("toki%d" % pb, xsem)

                        def f(te, xsem=xsem, xr_ld=xr_ld):
                            wt(te, xsem, xr_ld)
                            use_res(te, "ps_tx")
                            ins = None
                            for u in range(2):
                                for j in range(8):
                                    c0 = u * 1024 + j * 128
                                    ins = te.transpose(
                                        ps_tx[:, c0:c0 + 128],
                                        xrow[:, c0:c0 + 128], ident_sb[:])
                            inc(ins, "pe")
                        TE(f)
                        tx_mm = n("pe")
                        set_res("xrow", "pe")

                        def f(v, s2=s2, tx_mm=tx_mm):
                            wt(v, "pe", tx_mm)
                            if s2 == 0:
                                use_res(v, "xg")
                            ins = None
                            for u in range(2):
                                s = 2 * s2 + u
                                ov = xg.rearrange("p (j w) -> p j w", w=512)[
                                    :, :, s * 128:(s + 1) * 128]
                                iv = ps_tx[:, u * 1024:(u + 1) * 1024]\
                                    .rearrange("p (j w) -> p j w", w=128)
                                ins = v.tensor_copy(ov, iv)
                            inc(ins, "dve")
                        VE(f)
                        set_res("ps_tx", "dve")
                    xg_done = n("dve")
                    flush_scat()

                    for hc in range(32):
                        hb = hc % 2

                        def f(te, hc=hc, hb=hb, xg_done=xg_done):
                            if hc == 0:
                                wt(te, "wld", w_done)
                                wt(te, "wbf", wbf_done)
                                wt(te, "dve", xg_done)
                                te.wait_ge(sems["dve"], o5_done)
                            use_res(te, "ps_h%d" % hb)
                            te.matmul(ps_h[:, hb * 512:(hb + 1) * 512],
                                      b1r_sb[0:1, hc * 128:(hc + 1) * 128],
                                      ones5[0:1, :], start=True, stop=False)
                            ins = None
                            for j in range(8):
                                ins = te.matmul(
                                    ps_h[:, hb * 512:(hb + 1) * 512],
                                    w1_sb[:, j * HDN + hc * 128:
                                          j * HDN + (hc + 1) * 128],
                                    xg[:, j * 512:(j + 1) * 512],
                                    start=False, stop=(j == 7))
                            inc(ins, "pe")
                        TE(f)
                        h_mm = n("pe")

                        def f(sc, hc=hc, hb=hb, h_mm=h_mm):
                            wt(sc, "pe", h_mm)
                            inc(sc.activation(
                                hT_g[:, hc * 512:(hc + 1) * 512],
                                ps_h[:, hb * 512:(hb + 1) * 512],
                                AF.Relu), "act")
                        SC(f)
                        set_res("ps_h%d" % hb, "act")
                    set_res("xg", "pe")
                    h_done = n("act")

                    for oc in range(8):
                        ob = oc % 2

                        def f(te, oc=oc, ob=ob, h_done=h_done):
                            if oc == 0:
                                wt(te, "act", h_done)
                            use_res(te, "ps_y%d" % ob)
                            te.matmul(ps_y[:, ob * 512:(ob + 1) * 512],
                                      b2r_sb[0:1, oc * 128:(oc + 1) * 128],
                                      ones5[0:1, :], start=True, stop=False)
                            ins = None
                            for hcc in range(32):
                                ins = te.matmul(
                                    ps_y[:, ob * 512:(ob + 1) * 512],
                                    w2_sb[:, hcc * 1024 + oc * 128:
                                          hcc * 1024 + (oc + 1) * 128],
                                    hT_g[:, hcc * 512:(hcc + 1) * 512],
                                    start=False, stop=(hcc == 31))
                            inc(ins, "pe")
                        TE(f)
                        y_mm = n("pe")

                        def f(v, oc=oc, ob=ob, y_mm=y_mm):
                            wt(v, "pe", y_mm)
                            if oc == 0:
                                use_res(v, "yT")
                            inc(v.tensor_copy(
                                yT_sb[:, oc * 512:(oc + 1) * 512],
                                ps_y[:, ob * 512:(ob + 1) * 512]), "dve")
                        VE(f)
                        set_res("ps_y%d" % ob, "dve")
                    yT_done = n("dve")

                    for s in range(4):
                        pb2 = s % 2

                        def f(te, s=s, pb2=pb2, yT_done=yT_done):
                            if s == 0:
                                wt(te, "dve", yT_done)
                            use_res(te, "ps_ty%d" % pb2)
                            ins = None
                            for oc in range(8):
                                ins = te.transpose(
                                    ps_ty[:, pb2 * 1024 + oc * 128:
                                          pb2 * 1024 + (oc + 1) * 128],
                                    yT_sb[:, oc * 512 + s * 128:
                                          oc * 512 + (s + 1) * 128],
                                    ident_sb[:])
                            inc(ins, "pe")
                        TE(f)
                        ty_mm = n("pe")

                        def f(v, s=s, pb2=pb2, ty_mm=ty_mm, tg_done=tg_done):
                            wt(v, "pe", ty_mm)
                            wt(v, "dve", tg_done)
                            use_res(v, "yr%d" % s)
                            inc(v.tensor_scalar(
                                yr_sb[:, s * 1024:(s + 1) * 1024],
                                ps_ty[:, pb2 * 1024:(pb2 + 1) * 1024],
                                gat[:, pb * 4 + s:pb * 4 + s + 1], None,
                                op0=ALU.mult), "dve")
                        VE(f)
                        yr_ev = n("dve")
                        set_res("ps_ty%d" % pb2, "dve")

                        def f(g, s=s, yr_ev=yr_ev, gg=gg,
                              ct_done=ct_done, pb=pb):
                            wt(g, "dve", yr_ev)
                            if gg == 0 and s == 0:
                                wt(g, "ycz")
                            wt(g, "ct%d" % pb, ct_done)
                            inc(g.indirect_dma_start(
                                ycon,
                                IndirectOffsetOnAxis(
                                    dsti[:, pb * 4 + s:pb * 4 + s + 1], 0),
                                yr_sb[:, s * 1024:(s + 1) * 1024], None),
                                "ym%d" % (s % 2), 16)
                        pend_scat.append((f, s, pb))
                    set_res("yT", "pe")
                flush_scat()

                def f(g):
                    wt(g, "ym0")
                    wt(g, "ym1")
                    inc(g.collective_compute(
                        "ReduceScatter", ALU.add,
                        replica_groups=[list(range(NCORES))],
                        ins=[ycon[0:8 * NTOK, :].opt()],
                        outs=[yrs.opt()]), "ccy")
                GP(f)

        join_all()
        last_use.clear()

        with ExitStack() as s4v:
            if phases >= 4 and MOE_V1:
                e4 = s4v.enter_context
                xf = e4(nc.sbuf_tensor("xf", [128, 2 * 1024], F32))
                af = e4(nc.sbuf_tensor("af", [128, 2 * 1024], BF16))
                yf = e4(nc.sbuf_tensor("yf", [128, 2 * 1024], BF16))
                of = e4(nc.sbuf_tensor("of", [128, 2 * 1024], F32))

                for gt in range(64):
                    bi = gt % 2

                    def f(sy, gt=gt, bi=bi):
                        use_res(sy, "xa%d" % bi)
                        inc(sy.dma_start(out=xf[:, bi * 1024:(bi + 1) * 1024],
                                         in_=x_io[gt * 128:(gt + 1) * 128, :]),
                            "ldf%d" % bi, 16)
                    SY(f)

                    def f(sc, gt=gt, bi=bi):
                        use_res(sc, "xa%d" % bi)
                        inc(sc.dma_start(out=af[:, bi * 1024:(bi + 1) * 1024],
                                         in_=attn_sc[gt * 128:
                                                     (gt + 1) * 128, :]),
                            "ldf%d" % bi, 16)
                    SC(f)

                    def f(g, gt=gt, bi=bi):
                        use_res(g, "xa%d" % bi)
                        inc(g.dma_start(out=yf[:, bi * 1024:(bi + 1) * 1024],
                                        in_=yrs[gt * 128:(gt + 1) * 128, :]),
                            "ldg%d" % bi, 16)
                    GP(f)
                    xa_ld = n("ldf%d" % bi)
                    yf_ld = n("ldg%d" % bi)

                    def f(v, gt=gt, bi=bi, xa_ld=xa_ld, yf_ld=yf_ld):
                        wt(v, "ldf%d" % bi, xa_ld)
                        wt(v, "ldg%d" % bi, yf_ld)
                        use_res(v, "of%d" % bi)
                        sl = slice(bi * 1024, (bi + 1) * 1024)
                        cs(v, v.tensor_tensor(of[:, sl], xf[:, sl],
                                              af[:, sl], op=ALU.add))
                        inc(v.tensor_tensor(of[:, sl], of[:, sl],
                                            yf[:, sl], op=ALU.add), "dve")
                    VE(f)
                    o_ev = n("dve")
                    set_res("xa%d" % bi, "dve")

                    def f(g, gt=gt, bi=bi, o_ev=o_ev):
                        wt(g, "dve", o_ev)
                        inc(g.dma_start(
                            out=out_io[gt * 128:(gt + 1) * 128, :],
                            in_=of[:, bi * 1024:(bi + 1) * 1024]),
                            "out%d" % bi, 16)
                    GP(f)
                    set_res("of%d" % bi, "out%d" % bi)

        with ExitStack() as s3:
            if phases >= 3 and not MOE_V1:
                e3 = s3.enter_context
                w1_sb = e3(nc.sbuf_tensor("w1_sb", [128, 8 * HDN], BF16))
                w2_sb = e3(nc.sbuf_tensor("w2_sb", [128, 32 * 1024], BF16))
                b1_sb = e3(nc.sbuf_tensor("b1_sb", [128, 32], F32))
                b2_sb = e3(nc.sbuf_tensor("b2_sb", [128, 8], F32))
                xng = e3(nc.sbuf_tensor("xng", [128, 2 * 8 * 512], BF16))
                hT_g = e3(nc.sbuf_tensor("hT_g", [128, 32 * 512], BF16))
                cb_sb = e3(nc.sbuf_tensor("cb_sb", [128, 2 * 512], F32))
                cr_sb = e3(nc.sbuf_tensor("cr_sb", [1, 2 * 512], BF16))
                y_sb = e3(nc.sbuf_tensor("y_sb", [128, 2 * 512], BF16))
                ps_h = e3(nc.psum_tensor("ps_h", [128, 1024], F32))
                ps_y = e3(nc.psum_tensor("ps_y", [128, 1024], F32))
                ps_cb = e3(nc.psum_tensor("ps_cb", [128, 512], F32))

                crow_flat = crow_sc.rearrange("e a b -> e (a b)")

                for ee in range(E):
                    def f(g):
                        use_res(g, "w12")
                        inc(g.dma_start(out=w1_sb[:], in_=w1_io[ee]), "cst", 16)
                        inc(g.dma_start(out=w2_sb[:], in_=w2_io[ee]), "cst", 16)
                        inc(g.dma_start(out=b1_sb[:], in_=b1_io[ee]), "cst", 16)
                        inc(g.dma_start(out=b2_sb[:], in_=b2_io[ee]), "cst", 16)
                    GP(f)
                    w_done = n("cst")

                    for g8 in range(16):
                        bi = g8 % 2

                        def f(sy):
                            use_res(sy, "xng%d" % bi, "cr%d" % bi)
                            for j in range(8):
                                inc(sy.dma_start(out=xng[:, (bi * 8 + j) * 512:
                                                         (bi * 8 + j + 1) * 512],
                                                 in_=xnT_sc[j, g8]), "ld", 16)
                            inc(sy.dma_start(out=cr_sb[:, bi * 512:(bi + 1) * 512],
                                             in_=crow_flat[ee:ee + 1,
                                                           g8 * 512:(g8 + 1) * 512]),
                                "ld", 16)
                        SY(f)
                        xg_ld = n("ld")

                        def f(te):
                            wt(te, "cst", w_done)
                            wt(te, "ld", xg_ld)
                            use_res(te, "ps_cb")
                            inc(te.matmul(ps_cb[:], ones_r[:],
                                          cr_sb[:, bi * 512:(bi + 1) * 512],
                                          start=True, stop=True), "pe")
                        TE(f)
                        cb_mm = n("pe")

                        def f(v):
                            wt(v, "pe", cb_mm)
                            use_res(v, "cb%d" % bi)
                            inc(v.tensor_copy(cb_sb[:, bi * 512:(bi + 1) * 512],
                                              ps_cb[:]), "dve")
                            cnts["vch"] += 1
                            v.nop().then_inc(sems["vch"], 1)
                        VE(f)
                        cb_done = n("dve")
                        cb_sync = n("vch")
                        set_res("ps_cb", "dve")

                        for hc in range(32):
                            hb = hc % 2

                            def f(te):
                                use_res(te, "ps_h%d" % hb)
                                for j in range(8):
                                    ins = te.matmul(
                                        ps_h[:, hb * 512:(hb + 1) * 512],
                                        w1_sb[:, j * HDN + hc * 128:
                                              j * HDN + (hc + 1) * 128],
                                        xng[:, (bi * 8 + j) * 512:
                                            (bi * 8 + j + 1) * 512],
                                        start=(j == 0), stop=(j == 7))
                                inc(ins, "pe")
                            TE(f)
                            h_mm = n("pe")

                            def f(sc):
                                wt(sc, "pe", h_mm)
                                inc(sc.activation(hT_g[:, hc * 512:(hc + 1) * 512],
                                                  ps_h[:, hb * 512:(hb + 1) * 512],
                                                  AF.Relu, bias=b1_sb[:, hc:hc + 1]),
                                    "act")
                            SC(f)
                            set_res("ps_h%d" % hb, "act")
                        h_done = n("act")
                        set_res("xng%d" % bi, "pe")
                        set_res("cr%d" % bi, "pe")

                        for oc in range(8):
                            ob = oc % 2

                            def f(te):
                                wt(te, "act", h_done)
                                use_res(te, "ps_y%d" % ob)
                                for hc in range(32):
                                    ins = te.matmul(
                                        ps_y[:, ob * 512:(ob + 1) * 512],
                                        w2_sb[:, hc * 1024 + oc * 128:
                                              hc * 1024 + (oc + 1) * 128],
                                        hT_g[:, hc * 512:(hc + 1) * 512],
                                        start=(hc == 0), stop=(hc == 31))
                                inc(ins, "pe")
                            TE(f)
                            y_mm = n("pe")

                            def f(v):
                                wt(v, "pe", y_mm)
                                wt(v, "dve", cb_done)
                                use_res(v, "y%d" % ob)
                                inc(v.scalar_tensor_tensor(
                                    y_sb[:, ob * 512:ob * 512 + 512],
                                    ps_y[:, ob * 512:(ob + 1) * 512],
                                    b2_sb[:, oc:oc + 1],
                                    cb_sb[:, bi * 512:(bi + 1) * 512],
                                    op0=ALU.add, op1=ALU.mult), "dve")
                            VE(f)
                            y_ev = n("dve")
                            set_res("ps_y%d" % ob, "dve")

                            def f(g):
                                wt(g, "dve", y_ev)
                                inc(g.dma_start(out=ffn_sc[oc, g8],
                                                in_=y_sb[:, ob * 512:ob * 512 + 512],
                                                accum_op=(ALU.add if ee > 0
                                                          else ALU.bypass)),
                                    "sto", 16)
                            GP(f)
                            set_res("y%d" % ob, "sto")
                    set_res("w12", "pe")

        join_all()
        last_use.clear()

        with ExitStack() as s4:
            if phases >= 4 and not MOE_V1:
                e4 = s4.enter_context
                xf = e4(nc.sbuf_tensor("xf", [128, 2 * 1024], F32))
                af = e4(nc.sbuf_tensor("af", [128, 2 * 1024], BF16))
                ff = e4(nc.sbuf_tensor("ff", [128, 2 * 8 * 512], BF16))
                fr = e4(nc.sbuf_tensor("fr", [128, 2 * 1024], BF16))
                of = e4(nc.sbuf_tensor("of", [128, 2 * 1024], F32))
                ps_tf = e4(nc.psum_tensor("ps_tf", [128, 1024], BF16))

                for g8 in range(16):
                    bg = g8 % 2

                    def f(sy):
                        use_res(sy, "ff%d" % bg)
                        for oc in range(8):
                            inc(sy.dma_start(out=ff[:, (bg * 8 + oc) * 512:
                                                     (bg * 8 + oc + 1) * 512],
                                             in_=ffn_sc[oc, g8]), "ld", 16)
                    SY(f)
                    ff_ld = n("ld")

                    for tq in range(4):
                        gt = g8 * 4 + tq
                        bi = gt % 2

                        def f(sy):
                            use_res(sy, "xa%d" % bi)
                            inc(sy.dma_start(out=xf[:, bi * 1024:(bi + 1) * 1024],
                                             in_=x_io[gt * 128:(gt + 1) * 128, :]),
                                "ld", 16)
                            inc(sy.dma_start(out=af[:, bi * 1024:(bi + 1) * 1024],
                                             in_=attn_sc[gt * 128:(gt + 1) * 128, :]),
                                "ld", 16)
                        SY(f)
                        xa_ld = n("ld")

                        def f(te):
                            wt(te, "ld", ff_ld)
                            use_res(te, "ps_tf")
                            for oc in range(8):
                                ins = te.transpose(
                                    ps_tf[:, oc * 128:(oc + 1) * 128],
                                    ff[:, (bg * 8 + oc) * 512 + tq * 128:
                                       (bg * 8 + oc) * 512 + tq * 128 + 128],
                                    ident_sb[:])
                            inc(ins, "pe")
                        TE(f)
                        ftp = n("pe")
                        if tq == 3:
                            set_res("ff%d" % bg, "pe")

                        def f(v):
                            wt(v, "pe", ftp)
                            use_res(v, "fr%d" % bi)
                            inc(v.tensor_copy(fr[:, bi * 1024:(bi + 1) * 1024],
                                              ps_tf[:]), "dve")
                        VE(f)
                        set_res("ps_tf", "dve")

                        def f(v):
                            wt(v, "ld", xa_ld)
                            v.tensor_tensor(of[:, bi * 1024:(bi + 1) * 1024],
                                            xf[:, bi * 1024:(bi + 1) * 1024],
                                            af[:, bi * 1024:(bi + 1) * 1024],
                                            op=ALU.add)
                            inc(v.tensor_tensor(of[:, bi * 1024:(bi + 1) * 1024],
                                                of[:, bi * 1024:(bi + 1) * 1024],
                                                fr[:, bi * 1024:(bi + 1) * 1024],
                                                op=ALU.add), "dve")
                        VE(f)
                        o_ev = n("dve")
                        set_res("fr%d" % bi, "dve")
                        set_res("xa%d" % bi, "dve")

                        def f(g):
                            wt(g, "dve", o_ev)
                            inc(g.dma_start(out=out_io[gt * 128:(gt + 1) * 128, :],
                                            in_=of[:, bi * 1024:(bi + 1) * 1024]),
                                "outs", 16)
                        GP(f)

        if d_attn is not None:
            def f(g):
                for i in range(16):
                    inc(g.dma_start(out=d_attn[i * 512:(i + 1) * 512, :],
                                    in_=attn_sc[i * 512:(i + 1) * 512, :]),
                        "outs", 16)
            GP(f)
        if d_xn is not None:
            def f(g):
                dv = d_xn.rearrange("(c g p) w -> c g p w", c=8, g=16)
                for c in range(8):
                    for g8 in range(16):
                        inc(g.dma_start(out=dv[c, g8], in_=xnT_sc[c, g8]),
                            "outs", 16)
            GP(f)

        def f(g):
            wt(g, "outs")
            wt(g, "out0")
            wt(g, "out1")
        GP(f)

    return nc, dbg_outs


def _prep_inputs(inputs):
    Wq = np.asarray(inputs["L0_attn_Wq"], np.float32)
    Wk = np.asarray(inputs["L0_attn_Wk"], np.float32)
    Wv = np.asarray(inputs["L0_attn_Wv"], np.float32)
    Wo = np.asarray(inputs["L0_attn_Wo"], np.float32)
    wg = np.asarray(inputs["L0_ffn_wg"], np.float32)
    bq = np.asarray(inputs["L0_attn_bq"], np.float32)
    bk = np.asarray(inputs["L0_attn_bk"], np.float32)
    bv = np.asarray(inputs["L0_attn_bv"], np.float32)
    bo = np.asarray(inputs["L0_attn_bo"], np.float32)
    w1 = np.asarray(inputs["L0_ffn_w1"], np.float32)
    b1 = np.asarray(inputs["L0_ffn_b1"], np.float32)
    w2 = np.asarray(inputs["L0_ffn_w2"], np.float32)
    b2 = np.asarray(inputs["L0_ffn_b2"], np.float32)
    x = np.asarray(inputs["L0_attn_x"], np.float32)

    com = {}
    com["wq"] = np.ascontiguousarray(
        Wq.reshape(8, 128, 1024).transpose(1, 0, 2).reshape(128, 8192))
    kperm = np.concatenate([np.arange(0, 32, 2), np.arange(1, 32, 2),
                            np.arange(32, 64, 2), np.arange(33, 64, 2)])
    Wk = Wk[:, kperm]
    bk = bk[kperm]
    wkvg = np.concatenate([Wk, Wv, wg], axis=1)
    com["wkvg"] = np.ascontiguousarray(
        wkvg.reshape(8, 128, 136).transpose(1, 0, 2).reshape(128, 8 * 136))
    com["wo"] = np.ascontiguousarray(
        Wo.reshape(8, 128, 1024).transpose(1, 0, 2).reshape(128, 8192))
    com["bq"] = bq.reshape(1, 1024)
    com["bkv"] = np.concatenate([bk, bv, np.zeros(8, np.float32)]).reshape(1, 136)
    com["bo"] = np.ascontiguousarray(bo.reshape(8, 128).T)
    com["w1"] = np.ascontiguousarray(
        w1.reshape(E, 8, 128, HDN).transpose(0, 2, 1, 3).reshape(E, 128, 8 * HDN))
    com["w2"] = np.ascontiguousarray(
        w2.reshape(E, 32, 128, 1024).transpose(0, 2, 1, 3).reshape(E, 128, 32 * 1024))
    com["b1"] = np.ascontiguousarray(b1.reshape(E, 32, 128).transpose(0, 2, 1))
    com["b2"] = np.ascontiguousarray(b2.reshape(E, 8, 128).transpose(0, 2, 1))

    half = HD // 2
    theta = (1.0 / 10000.0 ** (np.arange(0, half, 2, dtype=np.float64) / half))
    hpos = np.arange(NH, dtype=np.float64)
    hc = np.cos(hpos[:, None] * theta[None, :])
    hs = np.sin(hpos[:, None] * theta[None, :])
    A = np.zeros((NH, 64)); Bc = np.zeros((NH, 64))
    A[:, 0:16] = hc; Bc[:, 0:16] = -hs
    A[:, 16:32] = hs; Bc[:, 16:32] = hc
    A[:, 32:48] = hc; Bc[:, 32:48] = -hs
    A[:, 48:64] = hs; Bc[:, 48:64] = hc
    com["cosT"] = np.broadcast_to(A.reshape(1, 1024), (128, 1024)
                                  ).astype(np.float32).copy()
    com["sinT"] = np.broadcast_to(Bc.reshape(1, 1024), (128, 1024)
                                  ).astype(np.float32).copy()
    com["identc"] = np.eye(128, dtype=np.float32)
    tri = np.zeros((128, 4, 512), np.float32)
    ts = np.arange(128)[:, None]
    tq = np.arange(512)[None, :]
    for j in range(4):
        tri[:, j, :] = np.where(128 * j + ts > tq, NEG, 0.0)
    com["tri"] = tri.reshape(128, 2048)
    com["ltri"] = (np.arange(128)[:, None] < np.arange(128)[None, :]
                   ).astype(np.float32)
    selz = np.zeros((16, 8, 128), np.float32)
    for c in range(8):
        selz[2 * c, c, 0:64] = 1.0
        selz[2 * c + 1, c, 64:128] = 1.0
    com["selz"] = selz.reshape(16, 1024)
    com["iotar"] = np.broadcast_to(
        (8.0 - np.arange(8, dtype=np.float32)), (128, 8)).copy()

    if MOE_V1:
        com_w1, com_w2 = com.pop("w1"), com.pop("w2")
        com_b1, com_b2 = com.pop("b1"), com.pop("b2")
    p_ = np.arange(128)[:, None]
    maps = []
    for c in range(NCORES):
        m = dict(com)
        m["x"] = np.ascontiguousarray(x[4 * c:4 * c + 4].reshape(NTOK, D))
        if MOE_V1:
            m["w1"] = com_w1[c:c + 1]
            m["w2"] = com_w2[c:c + 1]
            m["b1"] = com_b1[c:c + 1]
            m["b2"] = com_b2[c:c + 1]
            m["b1r"] = b1[c].reshape(1, HDN)
            m["b2r"] = b2[c].reshape(1, D)
            m["gidx"] = (c * NTOK + 128 * np.arange(64)[None, :] + p_
                         ).astype(np.float32)
            m["ebase"] = (c * ESLOT + 128 * np.arange(128)[None, :] + p_
                          ).astype(np.int32)
        selb = np.zeros((128, 16), np.float32)
        for ee in range(E):
            for cp in range(c):
                selb[cp * 16 + ee, ee] = 1.0
            for cp in range(NCORES):
                selb[cp * 16 + ee, 8 + ee] = 1.0
            for cp in range(c):
                selb[cp * 16 + 8 + ee, 8 + ee] = 1.0
        m["selb"] = selb
        maps.append(m)
    return maps


_CACHED = {}


def kernel(**inputs):
    dbg = _CACHED.get("dbg", ())
    key = ("nc", tuple(sorted(dbg)))
    if key not in _CACHED:
        _CACHED[key] = build_program(dbg)
    nc, dbg_outs = _CACHED[key]
    maps = _prep_inputs(inputs)
    res = run_bass_kernel_spmd(nc, maps, core_ids=list(range(NCORES)))
    outs = [r["out"] for r in res.results]
    full = np.concatenate(outs, axis=0).reshape(B, T, D)
    if dbg:
        _CACHED["last_results"] = res.results
    return full


# version 12
# speedup vs baseline: 1.0489x; 1.0489x over previous
import numpy as np
import ml_dtypes
import concourse.bass as bass
import concourse.mybir as mybir
from concourse.bass_utils import run_bass_kernel_spmd
from concourse.bass_core_rust import IndirectOffsetOnAxis
from contextlib import ExitStack

F32 = mybir.dt.float32
BF16 = mybir.dt.bfloat16
AF = mybir.ActivationFunctionType
ALU = mybir.AluOpType
AX = mybir.AxisListType

B, T, D = 32, 2048, 1024
NCORES = 8
NTOK = (B // NCORES) * T
NH, HD = 16, 64
E = 8
HDN = 4096
CAP = 16384
NEG = -480.0
MOE_V1 = True
ESLOT = 16384


def build_program(dbg=(), phases=4, p1sub=4, detect_races=False, serial=0):
    nc = bass.Bass("TRN2", target_bir_lowering=False,
                   detect_race_conditions=detect_races)

    x_io = nc.dram_tensor("x", [NTOK, D], F32, kind="ExternalInput").ap()
    wq_io = nc.dram_tensor("wq", [128, 8 * 1024], F32, kind="ExternalInput").ap()
    wkvg_io = nc.dram_tensor("wkvg", [128, 8 * 136], F32, kind="ExternalInput").ap()
    wo_io = nc.dram_tensor("wo", [128, 8 * 1024], F32, kind="ExternalInput").ap()
    bq_io = nc.dram_tensor("bq", [1, 1024], F32, kind="ExternalInput").ap()
    bkv_io = nc.dram_tensor("bkv", [1, 136], F32, kind="ExternalInput").ap()
    bo_io = nc.dram_tensor("bo", [128, 8], F32, kind="ExternalInput").ap()
    EW = 1 if MOE_V1 else E
    w1_io = nc.dram_tensor("w1", [EW, 128, 8 * HDN], F32, kind="ExternalInput").ap()
    w2_io = nc.dram_tensor("w2", [EW, 128, 32 * 1024], F32, kind="ExternalInput").ap()
    b1_io = nc.dram_tensor("b1", [EW, 128, 32], F32, kind="ExternalInput").ap()
    b2_io = nc.dram_tensor("b2", [EW, 128, 8], F32, kind="ExternalInput").ap()
    b1r_io = nc.dram_tensor("b1r", [1, HDN], F32, kind="ExternalInput").ap()
    b2r_io = nc.dram_tensor("b2r", [1, D], F32, kind="ExternalInput").ap()
    cos_io = nc.dram_tensor("cosT", [128, 1024], F32, kind="ExternalInput").ap()
    sin_io = nc.dram_tensor("sinT", [128, 1024], F32, kind="ExternalInput").ap()
    ident_io = nc.dram_tensor("identc", [128, 128], F32, kind="ExternalInput").ap()
    tri_io = nc.dram_tensor("tri", [128, 4 * 512], F32, kind="ExternalInput").ap()
    ltri_io = nc.dram_tensor("ltri", [128, 128], F32, kind="ExternalInput").ap()
    selz_io = nc.dram_tensor("selz", [16, 8 * 128], F32, kind="ExternalInput").ap()
    selb_io = nc.dram_tensor("selb", [128, 16], F32, kind="ExternalInput").ap()
    iotar_io = nc.dram_tensor("iotar", [128, 8], F32, kind="ExternalInput").ap()
    gidx_io = nc.dram_tensor("gidx", [128, 64], F32, kind="ExternalInput").ap()
    ebase_io = nc.dram_tensor("ebase", [128, 128],
                          mybir.dt.int32, kind="ExternalInput").ap()

    out_io = nc.dram_tensor("out", [NTOK, D], F32, kind="ExternalOutput").ap()

    xnT_sc = nc.dram_tensor("xnT_sc", [8, 16, 128, 512], BF16).ap()
    attn_sc = nc.dram_tensor("attn_sc", [NTOK, D], BF16).ap()
    ffn_sc = nc.dram_tensor("ffn_sc", [8, 16, 128, 512], BF16).ap()
    crow_sc = nc.dram_tensor("crow_sc", [E, 64, 128], BF16).ap()
    xnrow_sc = nc.dram_tensor("xnrow_sc", [NTOK, D], BF16).ap()
    xn_all = nc.dram_tensor("xn_all", [8 * NTOK, D], BF16,
                        addr_space="Shared").ap()
    buck_loc = nc.dram_tensor("buck_loc", [16 * ESLOT, 2], F32).ap()
    buck_all = nc.dram_tensor("buck_all", [8 * ESLOT, 2], F32,
                          addr_space="Shared").ap()
    ycon = nc.dram_tensor("ycon", [8 * NTOK + 128, D], BF16).ap()
    yrs = nc.dram_tensor("yrs", [NTOK, D], BF16).ap()
    zdram = nc.dram_tensor("zdram", [128 * 8192], BF16).ap()
    w1bf = nc.dram_tensor("w1bf", [128, 8 * HDN], BF16).ap()
    w2bf = nc.dram_tensor("w2bf", [128, 32 * 1024], BF16).ap()
    base_bounce = nc.dram_tensor("base_bounce", [16, 1], F32).ap()
    ag_in = nc.dram_tensor("ag_in", [16, 1], F32).ap()
    ag_out = nc.dram_tensor("ag_out", [128, 1], F32).ap()

    dbg_outs = {}

    def dbgt(name, shape):
        if name in dbg:
            dbg_outs[name] = nc.dram_tensor("dbg_" + name, shape, F32,
                                            kind="ExternalOutput").ap()
        return dbg_outs.get(name)

    d_lall = dbgt("lall", [128, 512])
    d_route = dbgt("route", [128, 8 * 64])
    d_cnt = dbgt("cnt", [128, 1])
    d_attn = dbgt("attn", [NTOK, D])
    d_xn = dbgt("xn", [8 * 16 * 128, 512])

    es = ExitStack()
    with es:
        en = es.enter_context

        sems = {}
        cnts = {}
        for nm in ["ld", "cst", "sto", "outs", "pe", "dve", "act", "cc",
                   "ldx0", "ldx1", "xns0", "xns1", "stg0", "stg1",
                   "aro0", "aro1", "ktd", "ldg0", "ldg1", "ldf0", "ldf1",
                   "out0", "out1", "agl", "vch", "sch",
                   "xnr0", "xnr1", "bsc", "ccx", "ccy", "ccb",
                   "pg0", "pg1", "ct0", "ct1", "xga", "xgb",
                   "ym0", "ym1", "ycz", "wbf", "wld",
                   "m_pe", "m_dve", "m_act", "m_gp", "m_sy"]:
            sems[nm] = en(nc.semaphore(nm))
            cnts[nm] = 0

        def inc(instr, nm, k=1):
            cnts[nm] += k
            return instr.then_inc(sems[nm], k)

        def n(nm):
            return cnts[nm]

        def wt(eng, nm, val=None):
            eng.wait_ge(sems[nm], cnts[nm] if val is None else val)

        def cs(eng, ins, nm="vch"):
            cnts[nm] += 1
            ins.then_inc(sems[nm], 1)
            eng.wait_ge(sems[nm], cnts[nm])

        last_use = {}

        def use_res(eng, *names):
            for nm in names:
                for s, v in last_use.get(nm, ()):
                    eng.wait_ge(sems[s], v)

        def set_res(name, *semnames):
            last_use[name] = [(s, cnts[s]) for s in semnames]

        def add_res(name, semname):
            last_use.setdefault(name, []).append((semname, cnts[semname]))

        blk = en(nc.Block())
        TE, VE, SC, GP, SY = blk.tensor, blk.vector, blk.scalar, blk.gpsimd, blk.sync

        def join_all():
            marks = [("tensor", "m_pe"), ("vector", "m_dve"), ("scalar", "m_act"),
                     ("gpsimd", "m_gp"), ("sync", "m_sy")]
            for engname, mnm in marks:
                def f(eng, mnm=mnm):
                    inc(eng.nop(), mnm)
                getattr(blk, engname)(f)
            tgt = {k: n(k) for k in ["m_pe", "m_dve", "m_act", "m_gp", "m_sy",
                                     "ld", "cst", "sto", "outs",
                                     "ldx0", "ldx1", "xns0", "xns1",
                                     "stg0", "stg1", "aro0", "aro1", "ktd",
                                     "ldg0", "ldg1", "ldf0", "ldf1",
                                     "out0", "out1", "agl",
                                     "xnr0", "xnr1", "bsc",
                                     "pg0", "pg1", "ct0", "ct1", "xga", "xgb",
                                     "ym0", "ym1", "ycz", "wbf", "wld",
                                     "cc", "ccx", "ccy", "ccb"]}
            for engname, _ in marks:
                def f(eng, tgt=tgt):
                    for k, v in tgt.items():
                        eng.wait_ge(sems[k], v)
                getattr(blk, engname)(f)

        ident_sb = en(nc.sbuf_tensor("ident_sb", [128, 128], BF16))
        ltri_sb = en(nc.sbuf_tensor("ltri_sb", [128, 128], F32))
        selb_sb = en(nc.sbuf_tensor("selb_sb", [128, 16], F32))
        iotar_sb = en(nc.sbuf_tensor("iotar_sb", [128, 8], F32))
        ones_r = en(nc.sbuf_tensor("ones_r", [1, 128], BF16))
        ones_rf = en(nc.sbuf_tensor("ones_rf", [1, 128], F32))
        ones_cf = en(nc.sbuf_tensor("ones_cf", [128, 1], F32))
        l_all = en(nc.sbuf_tensor("l_all", [128, 512], F32))
        i1_sb = en(nc.sbuf_tensor("i1_sb", [128, 64], F32))
        i2_sb = en(nc.sbuf_tensor("i2_sb", [128, 64], F32))
        gv1_sb = en(nc.sbuf_tensor("gv1_sb", [128, 64], F32))
        gv2_sb = en(nc.sbuf_tensor("gv2_sb", [128, 64], F32))
        gidx_sb = en(nc.sbuf_tensor("gidx_sb", [128, 64], F32))
        ebase_sb = en(nc.sbuf_tensor("ebase_sb", [128, 128], mybir.dt.int32))

        def f(g):
            for dst, src in [(ident_sb, ident_io), (ltri_sb, ltri_io),
                             (selb_sb, selb_io), (iotar_sb, iotar_io),
                             (gidx_sb, gidx_io), (ebase_sb, ebase_io)]:
                inc(g.dma_start(out=dst[:], in_=src), "cst", 16)
        GP(f)

        eps_sb = en(nc.sbuf_tensor("eps_sb", [128, 1], F32))

        def f(v):
            wt(v, "cst")
            v.memset(ones_r[:], 1.0)
            v.memset(ones_rf[:], 1.0)
            v.memset(eps_sb[:], 1e-5)
            inc(v.memset(ones_cf[:], 1.0), "dve")
        VE(f)
        consts_dve = n("dve")

        with ExitStack() as s1:
            e1 = s1.enter_context
            if MOE_V1:
                def f(g):
                    inc(g.dma_start(out=w1bf, in_=w1_io[0]), "wbf", 16)
                    inc(g.dma_start(out=w2bf, in_=w2_io[0]), "wbf", 16)
                GP(f)

                ztile = e1(nc.sbuf_tensor("ztile", [128, 1024], BF16))

                def f(v):
                    inc(v.memset(ztile[:], 0.0), "dve")
                VE(f)
                zt_done = n("dve")

                def f(sy):
                    wt(sy, "dve", zt_done)
                    for r in range(8):
                        inc(sy.dma_start(
                            out=zdram[r * 131072:(r + 1) * 131072],
                            in_=ztile[:]), "ycz", 16)
                    wt(sy, "ycz")
                    yflat = ycon.rearrange("a c -> (a c)")
                    CH = 128 * 8192
                    for r in range(64):
                        inc(sy.dma_start(out=yflat[r * CH:(r + 1) * CH],
                                         in_=zdram), "ycz", 16)
                    inc(sy.dma_start(out=yflat[64 * CH:64 * CH + 131072],
                                     in_=zdram[0:131072]), "ycz", 16)
                SY(f)
            wq_sb = e1(nc.sbuf_tensor("wq_sb", [128, 8 * 1024], BF16))
            wkvg_sb = e1(nc.sbuf_tensor("wkvg_sb", [128, 8 * 136], BF16))
            wo_sb = e1(nc.sbuf_tensor("wo_sb", [128, 8 * 1024], BF16))
            bq_sb = e1(nc.sbuf_tensor("bq_sb", [1, 1024], BF16))
            bkv_sb = e1(nc.sbuf_tensor("bkv_sb", [1, 136], BF16))
            bo_sb = e1(nc.sbuf_tensor("bo_sb", [128, 8], F32))
            cos_sb = e1(nc.sbuf_tensor("cos_sb", [128, 1024], F32))
            sin_sb = e1(nc.sbuf_tensor("sin_sb", [128, 1024], F32))
            tri_sb = e1(nc.sbuf_tensor("tri_sb", [128, 2048], BF16))
            selz_sb = e1(nc.sbuf_tensor("selz_sb", [16, 1024], BF16))
            xbuf = e1(nc.sbuf_tensor("xbuf", [128, 2 * 1024], F32))
            sq_buf = e1(nc.sbuf_tensor("sq_buf", [128, 1024], F32))
            mrs = e1(nc.sbuf_tensor("mrs", [128, 16], F32))
            xn_bf = e1(nc.sbuf_tensor("xn_bf", [128, 2 * 1024], BF16))
            xnT_t = e1(nc.sbuf_tensor("xnT_t", [128, 2 * 1024], BF16))
            q_tile = e1(nc.sbuf_tensor("q_tile", [128, 2 * 1024], BF16))
            kv_t = e1(nc.sbuf_tensor("kv_t", [128, 2 * 136], F32))
            k_seq = e1(nc.sbuf_tensor("k_seq", [128, 1152], BF16))
            kT_seq = e1(nc.sbuf_tensor("kT_seq", [128, 2048], BF16))
            stag_sb = e1(nc.sbuf_tensor("stag_sb", [128, 2 * 512], BF16))
            vaug_seq = e1(nc.sbuf_tensor("vaug_seq", [128, 16 * 65], BF16))
            qT_seq = e1(nc.sbuf_tensor("qT_seq", [128, 8 * 2048], BF16))
            pT_buf = e1(nc.sbuf_tensor("pT_buf", [128, 16 * 512], BF16))
            attnT_un = e1(nc.sbuf_tensor("attnT_un", [128, 8 * 2048], BF16))
            z_seq = e1(nc.sbuf_tensor("z_seq", [16, 2048], BF16))
            rz_seq = e1(nc.sbuf_tensor("rz_seq", [16, 2048], BF16))
            rb_sb = e1(nc.sbuf_tensor("rb_sb", [128, 2 * 512], F32))
            anorm_g = e1(nc.sbuf_tensor("anorm_g", [128, 8 * 512], BF16))
            aoutT_g = e1(nc.sbuf_tensor("aoutT_g", [128, 8 * 512], BF16))
            arow_sb = e1(nc.sbuf_tensor("arow_sb", [128, 2 * 1024], BF16))

            ps_a = e1(nc.psum_tensor("ps_a", [128, 1024], F32))
            ps_kv = e1(nc.psum_tensor("ps_kv", [128, 512], F32))
            ps_s = e1(nc.psum_tensor("ps_s", [128, 1024], F32))
            ps_pv = e1(nc.psum_tensor("ps_pv", [128, 512], F32))
            ps_rb = e1(nc.psum_tensor("ps_rb", [128, 512], F32))
            ps_tp = e1(nc.psum_tensor("ps_tp", [128, 1024], BF16))

            def f(g):
                for dst, src in [(pT_buf[:, 0:8192], wq_io),
                                 (wkvg_sb[:], wkvg_io),
                                 (wo_sb[:], wo_io),
                                 (attnT_un[0:1, 0:1024], bq_io),
                                 (bkv_sb[:], bkv_io), (tri_sb[:], tri_io),
                                 (selz_sb[:], selz_io)]:
                    inc(g.dma_start(out=dst, in_=src), "cst", 16)
            GP(f)

            def f(sy):
                inc(sy.dma_start(out=bo_sb[:], in_=bo_io), "ld", 16)
                inc(sy.dma_start(out=cos_sb[:], in_=cos_io), "ld", 16)
                inc(sy.dma_start(out=sin_sb[:], in_=sin_io), "ld", 16)
            SY(f)
            w_cst0 = n("cst")
            w_ld = n("ld")

            def f(v):
                wt(v, "cst", w_cst0)
                wt(v, "ld", w_ld)
                raw = pT_buf[:, 0:8192]
                for g4, pab in enumerate([0, 0, 32, 32]):
                    off = g4 * 16
                    ov = wq_sb[:].rearrange("p (c h u) -> p c h u", c=8, h=16
                                            )[:, :, :, off:off + 16]
                    av = cos_sb[:].rearrange("p (h u) -> p h u", h=16
                                             )[:, :, off:off + 16].unsqueeze(1) \
                        .to_broadcast([128, 8, 16, 16])
                    bv = sin_sb[:].rearrange("p (h u) -> p h u", h=16
                                             )[:, :, off:off + 16].unsqueeze(1) \
                        .to_broadcast([128, 8, 16, 16])
                    e0 = raw.rearrange("p (c h u) -> p c h u", c=8, h=16
                                       )[:, :, :, pab:pab + 32:2]
                    e1v = raw.rearrange("p (c h u) -> p c h u", c=8, h=16
                                        )[:, :, :, pab + 1:pab + 32:2]
                    tmp = q_tile[:].rearrange("p (c h u) -> p c h u", c=8, h=16
                                              )[:, :, :, 0:16]
                    v.tensor_tensor(ov, e0, av, op=ALU.mult)
                    cs(v, v.tensor_tensor(tmp, e1v, bv, op=ALU.mult))
                    v.tensor_tensor(ov, ov, tmp, op=ALU.add)
                braw = attnT_un[0:1, 0:1024]
                for g4, pab in enumerate([0, 0, 32, 32]):
                    off = g4 * 16
                    ov = bq_sb[:].rearrange("p (h u) -> p h u", h=16
                                            )[:, :, off:off + 16]
                    av = cos_sb[0:1, :].rearrange("p (h u) -> p h u", h=16
                                                  )[:, :, off:off + 16]
                    bv = sin_sb[0:1, :].rearrange("p (h u) -> p h u", h=16
                                                  )[:, :, off:off + 16]
                    e0 = braw.rearrange("p (h u) -> p h u", h=16
                                        )[:, :, pab:pab + 32:2]
                    e1v = braw.rearrange("p (h u) -> p h u", h=16
                                         )[:, :, pab + 1:pab + 32:2]
                    tmp = q_tile[0:1, 0:256].rearrange("p (h u) -> p h u", h=16)
                    v.tensor_tensor(ov, e0, av, op=ALU.mult)
                    cs(v, v.tensor_tensor(tmp, e1v, bv, op=ALU.mult))
                    v.tensor_tensor(ov, ov, tmp, op=ALU.add)
                inc(v.engine_nop(), "dve")
            VE(f)
            wq_built = n("dve")
            set_res("pT_buf", "dve")
            set_res("q_tile0", "dve")
            w_cst = n("cst")

            dep = {}
            prev_rz = None

            for s in range(4):
                for t in range(16):
                    gt = s * 16 + t
                    bi = gt % 2
                    xsl = slice(bi * 1024, (bi + 1) * 1024)
                    d = dep.setdefault(gt, {})
                    p2 = dep.get(gt - 2 if serial < 2 else gt - 1)

                    def f(sy):
                        if p2 is not None:
                            sy.wait_ge(sems["dve"], p2["xn"])
                        inc(sy.dma_start(out=xbuf[:, xsl],
                                         in_=x_io[gt * 128:(gt + 1) * 128, :]),
                            "ldx%d" % bi, 16)
                    SY(f)
                    d["x"] = ("ldx%d" % bi, n("ldx%d" % bi))

                    def f(v):
                        v.wait_ge(sems[d["x"][0]], d["x"][1])
                        if p2 is not None:
                            v.wait_ge(sems["dve"], p2["xn"])
                        m0 = mrs[:, bi * 8:bi * 8 + 1]
                        s1v = mrs[:, bi * 8 + 2:bi * 8 + 3]
                        s2v = mrs[:, bi * 8 + 3:bi * 8 + 4]
                        ex2 = mrs[:, bi * 8 + 4:bi * 8 + 5]
                        var = mrs[:, bi * 8 + 5:bi * 8 + 6]
                        cs(v, v.tensor_tensor(sq_buf[:], xbuf[:, xsl],
                                              xbuf[:, xsl], op=ALU.mult))
                        cs(v, v.reduce_sum(s2v, sq_buf[:], axis=AX.X))
                        cs(v, v.reduce_sum(s1v, xbuf[:, xsl], axis=AX.X))
                        v.tensor_scalar(m0, s1v, 1.0 / 1024, None, op0=ALU.mult)
                        cs(v, v.tensor_scalar(ex2, s2v, 1.0 / 1024, None,
                                              op0=ALU.mult))
                        cs(v, v.tensor_tensor(var, m0, m0, op=ALU.mult))
                        inc(v.tensor_tensor(var, ex2, var, op=ALU.subtract), "dve")
                    VE(f)
                    d["var"] = n("dve")

                    def f(sc):
                        wt(sc, "dve", d["var"])
                        inc(sc.activation(mrs[:, bi * 8 + 6:bi * 8 + 7],
                                          mrs[:, bi * 8 + 5:bi * 8 + 6],
                                          AF.Sqrt, bias=eps_sb[:]), "act")
                    SC(f)
                    d["sqrt"] = n("act")

                    def f(v):
                        wt(v, "act", d["sqrt"])
                        use_res(v, "xn_bf%d" % bi)
                        cs(v, v.reciprocal(mrs[:, bi * 8 + 1:bi * 8 + 2],
                                           mrs[:, bi * 8 + 6:bi * 8 + 7]))
                        inc(v.tensor_scalar(xn_bf[:, xsl], xbuf[:, xsl],
                                            mrs[:, bi * 8:bi * 8 + 1],
                                            mrs[:, bi * 8 + 1:bi * 8 + 2],
                                            op0=ALU.subtract, op1=ALU.mult), "dve")
                    VE(f)
                    d["xn"] = n("dve")

                    if p1sub < 1:
                        if d_attn is not None:
                            def f(g):
                                wt(g, "dve", d["xn"])
                                inc(g.dma_start(
                                    out=d_attn[gt * 128:(gt + 1) * 128, :],
                                    in_=xn_bf[:, xsl]), "aro%d" % bi, 16)
                            GP(f)
                            set_res("xn_bf%d" % bi, "aro%d" % bi)
                        continue

                    def f(te):
                        wt(te, "cst", w_cst)
                        wt(te, "dve", d["xn"])
                        use_res(te, "ps_tp")
                        for j in range(8):
                            ins = te.transpose(ps_tp[:, j * 128:(j + 1) * 128],
                                               xn_bf[:, xsl][:, j * 128:(j + 1) * 128],
                                               ident_sb[:])
                        inc(ins, "pe")
                    TE(f)
                    d["xnT"] = n("pe")
                    set_res("xn_bf%d" % bi, "pe")

                    def f(v):
                        wt(v, "pe", d["xnT"])
                        use_res(v, "xnT_t%d" % bi)
                        inc(v.tensor_copy(xnT_t[:, bi * 1024:(bi + 1) * 1024],
                                          ps_tp[:]), "dve")
                    VE(f)
                    d["xnT_ev"] = n("dve")
                    set_res("ps_tp", "dve")

                    def f(g):
                        wt(g, "dve", d["xnT_ev"])
                        for j in range(8):
                            inc(g.dma_start(
                                out=xnT_sc[j, gt // 4, :,
                                           (gt % 4) * 128:(gt % 4) * 128 + 128],
                                in_=xnT_t[:, bi * 1024 + j * 128:
                                          bi * 1024 + (j + 1) * 128]),
                                "xns%d" % bi, 16)
                        if MOE_V1:
                            inc(g.dma_start(out=xnrow_sc[gt * 128:(gt + 1) * 128, :],
                                            in_=xn_bf[:, xsl]), "xnr%d" % bi, 16)
                    GP(f)
                    d["xnT_sto"] = ("xns%d" % bi, n("xns%d" % bi))
                    if MOE_V1:
                        add_res("xn_bf%d" % bi, "xnr%d" % bi)

                    def f(te):
                        wt(te, "ld", w_ld)
                        wt(te, "cst", w_cst)
                        te.wait_ge(sems["dve"], wq_built)
                        wt(te, "dve", d["xnT_ev"])
                        use_res(te, "ps_a0", "ps_a1", "ps_kv")
                        for grp in range(2):
                            for j in range(8):
                                te.matmul(ps_a[:, grp * 512:(grp + 1) * 512],
                                          xnT_t[:, bi * 1024 + j * 128:
                                                bi * 1024 + (j + 1) * 128],
                                          wq_sb[:, j * 1024 + grp * 512:
                                                j * 1024 + (grp + 1) * 512],
                                          start=(j == 0), stop=False)
                            inc(te.matmul(ps_a[:, grp * 512:(grp + 1) * 512],
                                          ones_r[:],
                                          bq_sb[:, grp * 512:(grp + 1) * 512],
                                          start=False, stop=True), "pe")
                        for j in range(8):
                            te.matmul(ps_kv[:, 0:136],
                                      xnT_t[:, bi * 1024 + j * 128:
                                            bi * 1024 + (j + 1) * 128],
                                      wkvg_sb[:, j * 136:(j + 1) * 136],
                                      start=(j == 0), stop=False)
                        inc(te.matmul(ps_kv[:, 0:136], ones_r[:], bkv_sb[:],
                                      start=False, stop=True), "pe")
                    TE(f)
                    d["proj"] = n("pe")
                    last_use["xnT_t%d" % bi] = [d["xnT_sto"],
                                                ("pe", d["proj"])]

                    def f(v):
                        wt(v, "pe", d["proj"])
                        use_res(v, "q_tile%d" % bi, "kv_t%d" % bi)
                        v.tensor_copy(q_tile[:, bi * 1024:(bi + 1) * 1024],
                                      ps_a[:])
                        cs(v, v.tensor_copy(kv_t[:, bi * 136:(bi + 1) * 136],
                                             ps_kv[:, 0:136]))
                        v.tensor_copy(l_all[:, gt * 8:(gt + 1) * 8],
                                      ps_kv[:, 128:136])
                        v.tensor_copy(k_seq[:, t * 64:(t + 1) * 64],
                                      kv_t[:, bi * 136:bi * 136 + 64])
                        v.tensor_copy(vaug_seq[:, t * 65:t * 65 + 64],
                                      kv_t[:, bi * 136 + 64:bi * 136 + 128])
                        inc(v.memset(vaug_seq[:, t * 65 + 64:t * 65 + 65], 1.0),
                            "dve")
                    VE(f)
                    d["proj_ev"] = n("dve")
                    set_res("ps_a0", "dve")
                    set_res("ps_a1", "dve")
                    set_res("ps_kv", "dve")
                    set_res("kv_t%d" % bi, "dve")

                    def f(te):
                        wt(te, "dve", d["proj_ev"])
                        use_res(te, "ps_tp")
                        for j in range(8):
                            ins = te.transpose(ps_tp[:, j * 128:(j + 1) * 128],
                                               q_tile[:, bi * 1024 + j * 128:
                                                      bi * 1024 + (j + 1) * 128],
                                               ident_sb[:])
                        inc(ins, "pe")
                    TE(f)
                    d["qT"] = n("pe")
                    set_res("q_tile%d" % bi, "pe")

                    def f(v):
                        wt(v, "pe", d["qT"])
                        for j in range(8):
                            ins = v.tensor_copy(qT_seq[:, j * 2048 + t * 128:
                                                       j * 2048 + (t + 1) * 128],
                                                ps_tp[:, j * 128:(j + 1) * 128])
                        inc(ins, "dve")
                    VE(f)
                    d["qT_ev"] = n("dve")
                    set_res("ps_tp", "dve")
                    if serial >= 1:
                        join_all()

                if p1sub < 2:
                    continue

                kr_done = n("dve")
                bp_k = n("pe")
                bd_k = n("dve")

                def f(te):
                    wt(te, "dve", kr_done)
                    use_res(te, "ps_tp")
                    for t2 in range(16):
                        if t2 >= 2:
                            te.wait_ge(sems["dve"], bd_k + t2 - 1)
                        inc(te.transpose(ps_tp[:, (t2 % 2) * 512:
                                               (t2 % 2) * 512 + 128],
                                         k_seq[:, t2 * 64:t2 * 64 + 128],
                                         ident_sb[:]), "pe")
                TE(f)

                def f(v):
                    for t2 in range(16):
                        v.wait_ge(sems["pe"], bp_k + t2 + 1)
                        inc(v.tensor_copy(kT_seq[0:64, t2 * 128:(t2 + 1) * 128],
                                          ps_tp[0:64, (t2 % 2) * 512:
                                                (t2 % 2) * 512 + 128]), "dve")
                VE(f)
                assert n("pe") == bp_k + 16
                kT_ev = n("dve")
                set_res("ps_tp", "dve")

                def f(sy):
                    wt(sy, "dve", kT_ev)
                    inc(sy.dma_start(out=kT_seq[64:128, :],
                                     in_=kT_seq[0:64, :]), "ktd", 16)
                SY(f)
                kT_done = ("ktd", n("ktd"))

                if MOE_V1 and s == 3:
                    def f(g):
                        wt(g, "xnr0")
                        wt(g, "xnr1")
                        inc(g.collective_compute(
                            "AllGather", ALU.bypass,
                            replica_groups=[list(range(NCORES))],
                            ins=[xnrow_sc.opt()], outs=[xn_all.opt()]), "ccx")
                    GP(f)

                if p1sub < 3:
                    continue
                for h in range(NH):
                    for g4 in range(4):
                        nchunk = 4 * g4 + 4
                        prev_exp = n("act")
                        prev_pv_d = last_use.get("pT_buf", ("dve", 0))
                        base_pe = n("pe")

                        def f(te):
                            te.wait_ge(sems[kT_done[0]], kT_done[1])
                            wt(te, "cst", w_cst)
                            use_res(te, "ps_s0", "ps_s1")
                            qb = (h % 2) * 64
                            for c in range(nchunk):
                                if c >= 2:
                                    te.wait_ge(sems["act"], prev_exp + c - 1)
                                bank = c % 2
                                diag = c >= 4 * g4
                                ins = te.matmul(
                                    ps_s[:, bank * 512:(bank + 1) * 512],
                                    kT_seq[qb:qb + 64, c * 128:(c + 1) * 128],
                                    qT_seq[qb:qb + 64,
                                           (h // 2) * 2048 + g4 * 512:
                                           (h // 2) * 2048 + (g4 + 1) * 512],
                                    start=True, stop=not diag)
                                if diag:
                                    j = c - 4 * g4
                                    ins = te.matmul(
                                        ps_s[:, bank * 512:(bank + 1) * 512],
                                        ident_sb[:],
                                        tri_sb[:, j * 512:(j + 1) * 512],
                                        start=False, stop=True)
                                inc(ins, "pe")
                        TE(f)
                        assert n("pe") == base_pe + nchunk

                        def f(sc):
                            use_res(sc, "pT_buf")
                            for c in range(nchunk):
                                sc.wait_ge(sems["pe"], base_pe + c + 1)
                                inc(sc.activation(pT_buf[:, c * 512:(c + 1) * 512],
                                                  ps_s[:, (c % 2) * 512:
                                                       (c % 2) * 512 + 512],
                                                  AF.Exp, scale=0.125), "act")
                        SC(f)
                        exp_done = n("act")
                        set_res("ps_s0", "act")
                        set_res("ps_s1", "act")

                        def f(te):
                            use_res(te, "ps_pv", "ps_pv2")
                            for c in range(nchunk):
                                te.wait_ge(sems["act"],
                                           exp_done - nchunk + c + 1)
                                ins = te.matmul(ps_pv[0:65, :],
                                                vaug_seq[:, c * 65:(c + 1) * 65],
                                                pT_buf[:, c * 512:(c + 1) * 512],
                                                start=(c == 0),
                                                stop=(c == nchunk - 1))
                            inc(ins, "pe")
                        TE(f)
                        pv_done = n("pe")
                        set_res("pT_buf", "pe")

                        sbi = (h * 4 + g4) % 2

                        def f(v):
                            wt(v, "pe", pv_done)
                            use_res(v, "stag%d" % sbi)
                            inc(v.tensor_copy(
                                stag_sb[0:65, sbi * 512:sbi * 512 + 512],
                                ps_pv[0:65, :]), "dve")
                        VE(f)
                        stag_ev = n("dve")
                        set_res("ps_pv", "dve")

                        def f(sy):
                            wt(sy, "dve", stag_ev)
                            if prev_rz is not None and h == 0 and g4 == 0:
                                sy.wait_ge(sems["dve"], prev_rz)
                            inc(sy.dma_start(
                                out=attnT_un[(h % 2) * 64:(h % 2) * 64 + 64,
                                             (h // 2) * 2048 + g4 * 512:
                                             (h // 2) * 2048 + (g4 + 1) * 512],
                                in_=stag_sb[0:64, sbi * 512:sbi * 512 + 512]),
                                "stg%d" % sbi, 16)
                            inc(sy.dma_start(
                                out=z_seq[h:h + 1, g4 * 512:(g4 + 1) * 512],
                                in_=stag_sb[64:65, sbi * 512:sbi * 512 + 512]),
                                "stg%d" % sbi, 16)
                        SY(f)
                        set_res("stag%d" % sbi, "stg%d" % sbi)

                if p1sub < 4:
                    continue

                def f(v):
                    wt(v, "act")
                    wt(v, "dve")
                    wt(v, "stg0")
                    wt(v, "stg1")
                    with nc.allow_low_precision(
                            reason="softmax denom recip on bf16 tiles; "
                                   "error ~0.4%% of a small attn term"):
                        inc(v.reciprocal(rz_seq[:], z_seq[:]), "dve")
                VE(f)
                rz_done = n("dve")
                prev_rz = rz_done

                for g4 in range(4):
                    for cch in range(8):
                        bi2 = cch % 2

                        def f(te):
                            wt(te, "dve", rz_done)
                            use_res(te, "ps_rb")
                            inc(te.matmul(ps_rb[:],
                                          selz_sb[:, cch * 128:(cch + 1) * 128],
                                          rz_seq[:, g4 * 512:(g4 + 1) * 512],
                                          start=True, stop=True), "pe")
                        TE(f)
                        rbmm = n("pe")

                        def f(v):
                            wt(v, "pe", rbmm)
                            use_res(v, "rb%d" % bi2)
                            cs(v, v.tensor_copy(rb_sb[:, bi2 * 512:(bi2 + 1) * 512],
                                                ps_rb[:]))
                            inc(v.tensor_tensor(
                                anorm_g[:, cch * 512:(cch + 1) * 512],
                                attnT_un[:, cch * 2048 + g4 * 512:
                                         cch * 2048 + (g4 + 1) * 512],
                                rb_sb[:, bi2 * 512:(bi2 + 1) * 512],
                                op=ALU.mult), "dve")
                        VE(f)
                        set_res("ps_rb", "dve")
                        set_res("rb%d" % bi2, "dve")
                    anorm_done = n("dve")

                    for co in range(8):
                        ob = co % 2

                        def f(te):
                            wt(te, "dve", anorm_done)
                            use_res(te, "ps_a%d" % ob)
                            for cch in range(8):
                                ins = te.matmul(
                                    ps_a[:, ob * 512:ob * 512 + 512],
                                    wo_sb[:, cch * 1024 + co * 128:
                                          cch * 1024 + co * 128 + 128],
                                    anorm_g[:, cch * 512:(cch + 1) * 512],
                                    start=(cch == 0), stop=(cch == 7))
                            inc(ins, "pe")
                        TE(f)
                        omm = n("pe")

                        def f(v):
                            wt(v, "pe", omm)
                            wt(v, "ld", w_ld)
                            inc(v.tensor_scalar(aoutT_g[:, co * 512:(co + 1) * 512],
                                                ps_a[:, ob * 512:ob * 512 + 512],
                                                bo_sb[:, co:co + 1], None,
                                                op0=ALU.add), "dve")
                        VE(f)
                        set_res("ps_a%d" % ob, "dve")
                    aout_done = n("dve")

                    for tq in range(4):
                        gt2 = s * 16 + g4 * 4 + tq
                        bi3 = tq % 2

                        def f(te):
                            wt(te, "dve", aout_done)
                            use_res(te, "ps_tp")
                            for co in range(8):
                                ins = te.transpose(
                                    ps_tp[:, co * 128:(co + 1) * 128],
                                    aoutT_g[:, co * 512 + tq * 128:
                                            co * 512 + tq * 128 + 128],
                                    ident_sb[:])
                            inc(ins, "pe")
                        TE(f)
                        rtp = n("pe")

                        def f(v):
                            wt(v, "pe", rtp)
                            use_res(v, "arow%d" % bi3)
                            inc(v.tensor_copy(arow_sb[:, bi3 * 1024:
                                                      (bi3 + 1) * 1024],
                                              ps_tp[:]), "dve")
                        VE(f)
                        rev = n("dve")
                        set_res("ps_tp", "dve")

                        def f(g):
                            wt(g, "dve", rev)
                            inc(g.dma_start(out=attn_sc[gt2 * 128:
                                                        (gt2 + 1) * 128, :],
                                            in_=arow_sb[:, bi3 * 1024:
                                                        (bi3 + 1) * 1024]),
                                "aro%d" % bi3, 16)
                        GP(f)
                        set_res("arow%d" % bi3, "aro%d" % bi3)

        join_all()
        last_use.clear()

        with ExitStack() as s2:
            if phases >= 2:
                e2 = s2.enter_context
                v1_sb = e2(nc.sbuf_tensor("v1r", [128, 64], F32))
                v2_sb = e2(nc.sbuf_tensor("v2r", [128, 64], F32))
                d3_sb = e2(nc.sbuf_tensor("d3r", [128, 512], F32))
                m3_sb = e2(nc.sbuf_tensor("m3r", [128, 512], F32))
                l2_sb = e2(nc.sbuf_tensor("l2r", [128, 512], F32))
                sg_sb = e2(nc.sbuf_tensor("sgr", [128, 64], F32))
                g1_sb = e2(nc.sbuf_tensor("g1r", [128, 64], F32))
                g2_sb = e2(nc.sbuf_tensor("g2r", [128, 64], F32))
                msk_sb = e2(nc.sbuf_tensor("mskr", [128, 16 * 64], F32))
                pos_sb = e2(nc.sbuf_tensor("posr", [128, 16 * 64], F32))
                p1_sb = e2(nc.sbuf_tensor("p1r", [128, 64], F32))
                p2_sb = e2(nc.sbuf_tensor("p2r", [128, 64], F32))
                cs_sb = e2(nc.sbuf_tensor("csr", [1, 16 * 64], F32))
                cnt_sb = e2(nc.sbuf_tensor("cntr", [1, 16], F32))
                scn_sb = e2(nc.sbuf_tensor("scnr", [1, 16 * 64], F32))
                zer_sb = e2(nc.sbuf_tensor("zerr", [1, 64], F32))
                agi_sb = e2(nc.sbuf_tensor("agir", [16, 1], F32))
                ago_sb = e2(nc.sbuf_tensor("agor", [128, 1], F32))
                base_sb = e2(nc.sbuf_tensor("baser", [16, 1], F32))
                baser_sb = e2(nc.sbuf_tensor("baserr", [1, 16], F32))
                ce_sb = e2(nc.sbuf_tensor("cer", [128, 128], BF16))
                ceT_sb = e2(nc.sbuf_tensor("ceTr", [64, 128], BF16))
                sc_sb = e2(nc.sbuf_tensor("scr", [128, 64], F32))
                vv_sb = e2(nc.sbuf_tensor("vvr", [128, 2 * 64], F32))
                ps_r = e2(nc.psum_tensor("ps_r", [128, 64], F32))
                ps_row = e2(nc.psum_tensor("ps_row", [1, 512], F32))
                ps_16 = e2(nc.psum_tensor("ps_16", [16, 64], F32))
                ps_ct = e2(nc.psum_tensor("ps_ct", [128, 128], BF16))

                def f(v):
                    l3 = l_all[:].rearrange("p (t e) -> p t e", e=8)
                    cs(v, v.tensor_reduce(v1_sb[:].unsqueeze(2), l3, axis=AX.X,
                                          op=ALU.max))
                    inc(v.tensor_tensor(d3_sb[:].rearrange("p (t e) -> p t e", e=8),
                                        l3, v1_sb[:].unsqueeze(2)
                                        .to_broadcast([128, 64, 8]),
                                        op=ALU.subtract), "dve")
                VE(f)

                def f(sc):
                    wt(sc, "dve")
                    inc(sc.sign(m3_sb[:], d3_sb[:]), "act")
                SC(f)

                def f(v):
                    wt(v, "act")
                    cs(v, v.tensor_scalar(m3_sb[:], m3_sb[:], 1.0, None,
                                          op0=ALU.add))
                    cs(v, v.tensor_tensor(d3_sb[:].rearrange("p (t e) -> p t e", e=8),
                                          m3_sb[:].rearrange("p (t e) -> p t e", e=8),
                                          iotar_sb[:].unsqueeze(1)
                                          .to_broadcast([128, 64, 8]), op=ALU.mult))
                    cs(v, v.tensor_reduce(i1_sb[:].unsqueeze(2),
                                          d3_sb[:].rearrange("p (t e) -> p t e", e=8),
                                          axis=AX.X, op=ALU.max))
                    cs(v, v.tensor_scalar(i1_sb[:], i1_sb[:], -1.0, 8.0,
                                          op0=ALU.mult, op1=ALU.add))
                    cs(v, v.tensor_scalar(d3_sb[:, 0:8], iotar_sb[:], -1.0, 8.0,
                                          op0=ALU.mult, op1=ALU.add))
                    inc(v.tensor_tensor(l2_sb[:].rearrange("p (t e) -> p t e", e=8),
                                        d3_sb[:, 0:8].unsqueeze(1)
                                        .to_broadcast([128, 64, 8]),
                                        i1_sb[:].unsqueeze(2)
                                        .to_broadcast([128, 64, 8]),
                                        op=ALU.subtract), "dve")
                VE(f)

                def f(sc):
                    wt(sc, "dve")
                    inc(sc.sign(m3_sb[:], l2_sb[:]), "act")
                SC(f)

                def f(v):
                    wt(v, "act")
                    cs(v, v.tensor_tensor(m3_sb[:], m3_sb[:], m3_sb[:], op=ALU.mult))
                    cs(v, v.tensor_scalar(m3_sb[:], m3_sb[:], -1.0, 1.0,
                                          op0=ALU.mult, op1=ALU.add))
                    cs(v, v.scalar_tensor_tensor(l2_sb[:], m3_sb[:], -1e9, l_all[:],
                                                 op0=ALU.mult, op1=ALU.add))
                    cs(v, v.tensor_reduce(v2_sb[:].unsqueeze(2),
                                          l2_sb[:].rearrange("p (t e) -> p t e", e=8),
                                          axis=AX.X, op=ALU.max))
                    inc(v.tensor_tensor(d3_sb[:].rearrange("p (t e) -> p t e", e=8),
                                        l2_sb[:].rearrange("p (t e) -> p t e", e=8),
                                        v2_sb[:].unsqueeze(2)
                                        .to_broadcast([128, 64, 8]),
                                        op=ALU.subtract), "dve")
                VE(f)

                def f(sc):
                    wt(sc, "dve")
                    inc(sc.sign(m3_sb[:], d3_sb[:]), "act")
                SC(f)

                def f(v):
                    wt(v, "act")
                    cs(v, v.tensor_scalar(m3_sb[:], m3_sb[:], 1.0, None,
                                          op0=ALU.add))
                    cs(v, v.tensor_tensor(d3_sb[:].rearrange("p (t e) -> p t e", e=8),
                                          m3_sb[:].rearrange("p (t e) -> p t e", e=8),
                                          iotar_sb[:].unsqueeze(1)
                                          .to_broadcast([128, 64, 8]), op=ALU.mult))
                    cs(v, v.tensor_reduce(i2_sb[:].unsqueeze(2),
                                          d3_sb[:].rearrange("p (t e) -> p t e", e=8),
                                          axis=AX.X, op=ALU.max))
                    cs(v, v.tensor_scalar(i2_sb[:], i2_sb[:], -1.0, 8.0,
                                          op0=ALU.mult, op1=ALU.add))
                    inc(v.tensor_tensor(sg_sb[:], v1_sb[:], v2_sb[:],
                                        op=ALU.subtract), "dve")
                VE(f)

                def f(sc):
                    wt(sc, "dve")
                    inc(sc.activation(g1_sb[:], sg_sb[:], AF.Sigmoid), "act")
                SC(f)

                def f(v):
                    wt(v, "act")
                    inc(v.tensor_scalar(g2_sb[:], g1_sb[:], -1.0, 1.0,
                                        op0=ALU.mult, op1=ALU.add), "dve")
                VE(f)

                for k in range(2):
                    isrc = i1_sb if k == 0 else i2_sb
                    for ee in range(E):
                        ix = k * 8 + ee

                        def f(v):
                            inc(v.tensor_scalar(sc_sb[:], isrc[:], float(ee), None,
                                                op0=ALU.subtract), "dve")
                        VE(f)

                        def f(sc):
                            wt(sc, "dve")
                            inc(sc.sign(msk_sb[:, ix * 64:(ix + 1) * 64], sc_sb[:]),
                                "act")
                        SC(f)

                        def f(v):
                            wt(v, "act")
                            mv = msk_sb[:, ix * 64:(ix + 1) * 64]
                            cs(v, v.tensor_tensor(mv, mv, mv, op=ALU.mult))
                            inc(v.tensor_scalar(mv, mv, -1.0, 1.0,
                                                op0=ALU.mult, op1=ALU.add), "dve")
                        VE(f)
                        msk_done = n("dve")

                        def f(te):
                            wt(te, "dve", msk_done)
                            use_res(te, "ps_row")
                            inc(te.matmul(ps_row[:, 0:64], ones_cf[:],
                                          msk_sb[:, ix * 64:(ix + 1) * 64],
                                          start=True, stop=True), "pe")
                        TE(f)
                        cs_mm = n("pe")

                        def f(v):
                            wt(v, "pe", cs_mm)
                            cs(v, v.tensor_copy(cs_sb[:, ix * 64:(ix + 1) * 64],
                                                ps_row[:, 0:64]))
                            inc(v.reduce_sum(cnt_sb[:, ix:ix + 1],
                                             cs_sb[:, ix * 64:(ix + 1) * 64],
                                             axis=AX.X), "dve")
                        VE(f)
                        set_res("ps_row", "dve")
                cnts_done = n("dve")

                def f(te):
                    wt(te, "dve", cnts_done)
                    inc(te.matmul(ps_16[0:16, 0:1], cnt_sb[:], ones_rf[:, 0:1],
                                  start=True, stop=True), "pe")
                TE(f)
                tr_cnt = n("pe")

                def f(v):
                    wt(v, "pe", tr_cnt)
                    v.memset(zer_sb[:], 0.0)
                    inc(v.tensor_copy(agi_sb[:], ps_16[0:16, 0:1]), "dve")
                VE(f)
                set_res("ps_16", "dve")

                def f(g):
                    wt(g, "dve")
                    inc(g.dma_start(out=ag_in, in_=agi_sb[:]), "sto", 16)
                    wt(g, "sto")
                    inc(g.collective_compute(
                        "AllGather", ALU.bypass,
                        replica_groups=[list(range(NCORES))],
                        ins=[ag_in.opt()], outs=[ag_out.opt()]), "cc")
                    wt(g, "cc")
                    inc(g.dma_start(out=ago_sb[:], in_=ag_out), "agl", 16)
                GP(f)
                ag_done = n("agl")

                def f(te):
                    wt(te, "agl", ag_done)
                    use_res(te, "ps_16")
                    inc(te.matmul(ps_16[0:16, 0:1], selb_sb[:], ago_sb[:],
                                  start=True, stop=True), "pe")
                TE(f)
                base_mm = n("pe")

                def f(v):
                    wt(v, "pe", base_mm)
                    inc(v.tensor_copy(base_sb[:], ps_16[0:16, 0:1]), "dve")
                VE(f)
                base_done = n("dve")

                def f(g):
                    wt(g, "dve", base_done)
                    inc(g.dma_start(out=base_bounce, in_=base_sb[:]), "sto", 16)
                    wt(g, "sto")
                    inc(g.dma_start(out=baser_sb[:],
                                    in_=base_bounce.rearrange("a b -> (a b)")
                                    .unsqueeze(0)), "agl", 16)
                GP(f)
                baser_done = n("agl")

                for k in range(2):
                    for ee in range(E):
                        ix = k * 8 + ee

                        def f(v):
                            wt(v, "agl", baser_done)
                            cs(v, v.tensor_tensor_scan(
                                scn_sb[:, ix * 64:(ix + 1) * 64],
                                cs_sb[:, ix * 64:(ix + 1) * 64],
                                zer_sb[:],
                                baser_sb[0:1, ix:ix + 1],
                                op0=ALU.add, op1=ALU.add))
                            inc(v.tensor_tensor(scn_sb[:, ix * 64:(ix + 1) * 64],
                                                scn_sb[:, ix * 64:(ix + 1) * 64],
                                                cs_sb[:, ix * 64:(ix + 1) * 64],
                                                op=ALU.subtract), "dve")
                        VE(f)
                        scn_done = n("dve")

                        def f(te):
                            wt(te, "dve", scn_done)
                            use_res(te, "ps_r")
                            te.matmul(ps_r[:], ltri_sb[:],
                                      msk_sb[:, ix * 64:(ix + 1) * 64],
                                      start=True, stop=False)
                            inc(te.matmul(ps_r[:], ones_rf[:],
                                          scn_sb[:, ix * 64:(ix + 1) * 64],
                                          start=False, stop=True), "pe")
                        TE(f)
                        pos_mm = n("pe")

                        def f(v):
                            wt(v, "pe", pos_mm)
                            inc(v.tensor_copy(pos_sb[:, ix * 64:(ix + 1) * 64],
                                              ps_r[:]), "dve")
                        VE(f)
                        set_res("ps_r", "dve")

                def f(v):
                    for k, pk in enumerate([p1_sb, p2_sb]):
                        cs(v, v.memset(pk[:], 0.0))
                        for ee in range(E):
                            ix = k * 8 + ee
                            cs(v, v.tensor_tensor(
                                sc_sb[:],
                                msk_sb[:, ix * 64:(ix + 1) * 64],
                                pos_sb[:, ix * 64:(ix + 1) * 64],
                                op=ALU.mult))
                            cs(v, v.tensor_tensor(pk[:], pk[:], sc_sb[:],
                                                  op=ALU.add))
                    v.tensor_scalar(vv_sb[:, 0:64], p1_sb[:], CAP - 0.5, None,
                                    op0=ALU.subtract)
                    inc(v.tensor_scalar(vv_sb[:, 64:128], p2_sb[:], CAP - 0.5, None,
                                        op0=ALU.subtract), "dve")
                VE(f)

                def f(sc):
                    wt(sc, "dve")
                    inc(sc.sign(d3_sb[:, 0:128], vv_sb[:]), "act")
                SC(f)

                def f(v):
                    wt(v, "act")
                    cs(v, v.tensor_scalar(d3_sb[:, 0:128], d3_sb[:, 0:128],
                                          -0.5, 0.5, op0=ALU.mult, op1=ALU.add))
                    v.tensor_tensor(gv1_sb[:], g1_sb[:], d3_sb[:, 0:64], op=ALU.mult)
                    inc(v.tensor_tensor(gv2_sb[:], g2_sb[:], d3_sb[:, 64:128],
                                        op=ALU.mult), "dve")
                VE(f)
                gv_done = n("dve")

                if MOE_V1:
                    spk = msk_sb[:, 0:128]
                    dstp = msk_sb[:, 128:256]
                    pr1 = msk_sb[:, 256:384]
                    pr2 = msk_sb[:, 384:512]
                    dsti = e2(nc.sbuf_tensor("dsti", [128, 128],
                                             mybir.dt.int32))
                    zbig = e2(nc.sbuf_tensor("zbig", [128, 4096], F32))

                    def f(v):
                        wt(v, "dve", gv_done)
                        for k2, (ik, pk, vo) in enumerate(
                                [(i1_sb, p1_sb, 0), (i2_sb, p2_sb, 64)]):
                            cs(v, v.tensor_scalar_min(spk[:, vo:vo + 64],
                                                      pk[:], 16383.0))
                            cs(v, v.tensor_scalar(sc_sb[:], ik[:],
                                                  float(ESLOT), None,
                                                  op0=ALU.mult))
                            cs(v, v.tensor_tensor(spk[:, vo:vo + 64],
                                                  spk[:, vo:vo + 64],
                                                  sc_sb[:], op=ALU.add))
                            cs(v, v.tensor_scalar(sc_sb[:],
                                                  d3_sb[:, vo:vo + 64],
                                                  -131072.0, 131072.0,
                                                  op0=ALU.mult, op1=ALU.add))
                            cs(v, v.tensor_tensor(dstp[:, vo:vo + 64],
                                                  spk[:, vo:vo + 64],
                                                  sc_sb[:], op=ALU.add))
                        cs(v, v.tensor_scalar(sc_sb[:], gidx_sb[:], 1.0, None,
                                              op0=ALU.add))
                        for prk, gvk in [(pr1, gv1_sb), (pr2, gv2_sb)]:
                            cs(v, v.tensor_copy(
                                prk.rearrange("p (t c) -> p t c", c=2)[:, :, 0],
                                sc_sb[:]))
                            cs(v, v.tensor_copy(
                                prk.rearrange("p (t c) -> p t c", c=2)[:, :, 1],
                                gvk[:]))
                        v.memset(zbig[:], 0.0)
                        inc(v.engine_nop(), "dve")
                    VE(f)
                    v1r_done = n("dve")

                    def f(g):
                        wt(g, "dve", v1r_done)
                        inc(g.dma_start(out=dsti[:], in_=dstp), "bsc", 16)
                        inc(g.dma_start(
                            out=buck_loc.rearrange("(p r) c -> p (r c)", p=128),
                            in_=zbig[:]), "bsc", 16)
                        wt(g, "bsc")
                        for k2, prk in [(0, pr1), (1, pr2)]:
                            for T2 in range(64):
                                col = k2 * 64 + T2
                                inc(g.indirect_dma_start(
                                    buck_loc,
                                    IndirectOffsetOnAxis(dsti[:, col:col + 1], 0),
                                    prk[:, 2 * T2:2 * T2 + 2], None),
                                    "bsc", 16)
                        wt(g, "bsc")
                        inc(g.collective_compute(
                            "AllReduce", ALU.add,
                            replica_groups=[list(range(NCORES))],
                            ins=[buck_loc[0:8 * ESLOT, :].opt()],
                            outs=[buck_all.opt()]), "ccb")
                    GP(f)

                for ee in (range(E) if not MOE_V1 else []):
                    def f(v):
                        wt(v, "dve", gv_done)
                        use_res(v, "ce")
                        v.tensor_tensor(sc_sb[:], gv1_sb[:],
                                        msk_sb[:, ee * 64:(ee + 1) * 64],
                                        op=ALU.mult)
                        cs(v, v.tensor_tensor(d3_sb[:, 0:64], gv2_sb[:],
                                              msk_sb[:, (8 + ee) * 64:
                                                     (9 + ee) * 64],
                                              op=ALU.mult))
                        cs(v, v.tensor_tensor(sc_sb[:], sc_sb[:], d3_sb[:, 0:64],
                                              op=ALU.add))
                        if ee == 0:
                            v.memset(ce_sb[:, 64:128], 0.0)
                        inc(v.tensor_copy(ce_sb[:, 0:64], sc_sb[:]), "dve")
                    VE(f)
                    ce_done = n("dve")

                    def f(te):
                        wt(te, "dve", ce_done)
                        use_res(te, "ps_ct")
                        inc(te.transpose(ps_ct[:], ce_sb[:], ident_sb[:]), "pe")
                    TE(f)
                    ct_mm = n("pe")

                    def f(v):
                        wt(v, "pe", ct_mm)
                        use_res(v, "ceT")
                        inc(v.tensor_copy(ceT_sb[:], ps_ct[0:64, :]), "dve")
                    VE(f)
                    ctev = n("dve")
                    set_res("ps_ct", "dve")
                    set_res("ce", "dve")

                    def f(g):
                        wt(g, "dve", ctev)
                        inc(g.dma_start(out=crow_sc[ee], in_=ceT_sb[:]), "sto", 16)
                    GP(f)
                    set_res("ceT", "sto")

                if d_lall is not None:
                    def f(g):
                        wt(g, "dve")
                        inc(g.dma_start(out=d_lall, in_=l_all[:]), "outs", 16)
                    GP(f)
                if d_route is not None:
                    def f(g):
                        wt(g, "dve")
                        for j, src in enumerate([i1_sb, i2_sb, gv1_sb, gv2_sb,
                                                 p1_sb, p2_sb, v1_sb, v2_sb]):
                            inc(g.dma_start(out=d_route[:, j * 64:(j + 1) * 64],
                                            in_=src[:]), "outs", 16)
                    GP(f)
                if d_cnt is not None:
                    def f(g):
                        wt(g, "agl")
                        inc(g.dma_start(out=d_cnt, in_=ago_sb[:]), "outs", 16)
                    GP(f)

        join_all()
        last_use.clear()

        with ExitStack() as s3:
            if phases >= 3 and MOE_V1:
                e3 = s3.enter_context
                I32 = mybir.dt.int32
                w1_sb = e3(nc.sbuf_tensor("w1_sb", [128, 8 * HDN], BF16))
                w2_sb = e3(nc.sbuf_tensor("w2_sb", [128, 32 * 1024], BF16))
                xrow = e3(nc.sbuf_tensor("xrow", [128, 2 * 1024], BF16))
                xg = e3(nc.sbuf_tensor("xg", [128, 8 * 512], BF16))
                hT_g = e3(nc.sbuf_tensor("hT_g", [128, 32 * 512], BF16))
                yT_sb = e3(nc.sbuf_tensor("yT_sb", [128, 8 * 512], BF16))
                yr_sb = e3(nc.sbuf_tensor("yr_sb", [128, 4 * 1024], BF16))
                prg = e3(nc.sbuf_tensor("prg", [128, 16], F32))
                tokf = e3(nc.sbuf_tensor("tokf", [128, 8], F32))
                toki = e3(nc.sbuf_tensor("toki", [128, 8], I32))
                gat = e3(nc.sbuf_tensor("gat", [128, 8], F32))
                dstf = e3(nc.sbuf_tensor("dstf3", [128, 8], F32))
                b1r_sb = e3(nc.sbuf_tensor("b1r_sb", [1, HDN], BF16))
                b2r_sb = e3(nc.sbuf_tensor("b2r_sb", [1, D], BF16))
                ones5 = e3(nc.sbuf_tensor("ones5", [1, 512], BF16))
                dsti = e3(nc.sbuf_tensor("dsti3", [128, 8], I32))
                ps_h = e3(nc.psum_tensor("ps_h", [128, 1024], F32))
                ps_y = e3(nc.psum_tensor("ps_y", [128, 1024], F32))
                ps_tx = e3(nc.psum_tensor("ps_tx", [128, 2048], BF16))
                ps_ty = e3(nc.psum_tensor("ps_ty", [128, 2048], BF16))

                def f(sy):
                    wt(sy, "wbf")
                    inc(sy.dma_start(out=w1_sb[:], in_=w1bf), "wld", 16)
                SY(f)

                def f(sc):
                    wt(sc, "wbf")
                    inc(sc.dma_start(out=w2_sb[:], in_=w2bf), "wld", 16)
                SC(f)

                def f(g):
                    inc(g.dma_start(out=b1r_sb[:], in_=b1r_io), "wbf", 16)
                    inc(g.dma_start(out=b2r_sb[:], in_=b2r_io), "wbf", 16)
                GP(f)
                w_done = n("wld")
                wbf_done = n("wbf")

                def f(v):
                    inc(v.memset(ones5[:], 1.0), "dve")
                VE(f)
                o5_done = n("dve")

                pend_scat = []

                def flush_scat():
                    for fn, s2, pb2 in pend_scat:
                        GP(fn)
                        set_res("yr%d" % s2, "ym%d" % (s2 % 2))
                    if pend_scat:
                        pb2 = pend_scat[0][2]
                        add_res("toki%d" % pb2, "ym0")
                        add_res("toki%d" % pb2, "ym1")
                    del pend_scat[:]

                for gg in range(32):
                    pb = gg % 2
                    q0 = 4 * gg

                    def f(g):
                        use_res(g, "prg%d" % pb)
                        for s in range(4):
                            inc(g.indirect_dma_start(
                                prg[:, pb * 8 + 2 * s:pb * 8 + 2 * s + 2],
                                None, buck_all,
                                IndirectOffsetOnAxis(
                                    ebase_sb[:, q0 + s:q0 + s + 1], 0)),
                                "pg%d" % pb, 16)
                    GP(f)
                    pg_done = n("pg%d" % pb)

                    def f(v):
                        wt(v, "pg%d" % pb, pg_done)
                        use_res(v, "tok%d" % pb)
                        pv = prg[:, pb * 8:(pb + 1) * 8].rearrange(
                            "p (s c) -> p s c", c=2)
                        dh = dstf[:, pb * 4:(pb + 1) * 4]
                        th = tokf[:, pb * 4:(pb + 1) * 4]
                        cs(v, v.tensor_scalar(th, pv[:, :, 0], -1.0, 0.0,
                                              op0=ALU.add, op1=ALU.max))
                        cs(v, v.tensor_scalar_min(dh, pv[:, :, 0], 1.0))
                        cs(v, v.tensor_scalar(dh, dh, -65536.0, 65536.0,
                                              op0=ALU.mult, op1=ALU.add))
                        cs(v, v.tensor_tensor(dh, dh, th, op=ALU.add))
                        inc(v.tensor_copy(gat[:, pb * 4:(pb + 1) * 4],
                                          pv[:, :, 1]), "dve")
                    VE(f)
                    tg_done = n("dve")
                    set_res("prg%d" % pb, "dve")

                    def f(g):
                        wt(g, "dve", tg_done)
                        use_res(g, "toki%d" % pb)
                        inc(g.dma_start(out=toki[:, pb * 4:(pb + 1) * 4],
                                        in_=tokf[:, pb * 4:(pb + 1) * 4]),
                            "ct%d" % pb, 16)
                        inc(g.dma_start(out=dsti[:, pb * 4:(pb + 1) * 4],
                                        in_=dstf[:, pb * 4:(pb + 1) * 4]),
                            "ct%d" % pb, 16)
                    GP(f)
                    ct_done = n("ct%d" % pb)
                    set_res("tok%d" % pb, "ct%d" % pb)

                    for s2 in range(2):
                        xsem = "xga" if s2 == 0 else "xgb"

                        def f(g, s2=s2, xsem=xsem, ct_done=ct_done):
                            wt(g, "ct%d" % pb, ct_done)
                            use_res(g, "xrow")
                            for u in range(2):
                                s = 2 * s2 + u
                                inc(g.indirect_dma_start(
                                    xrow[:, u * 1024:(u + 1) * 1024], None,
                                    xn_all,
                                    IndirectOffsetOnAxis(
                                        toki[:, pb * 4 + s:pb * 4 + s + 1],
                                        0)), xsem, 16)
                        GP(f)
                        xr_ld = n(xsem)
                        if s2 == 0:
                            set_res("toki%d" % pb, xsem)
                        else:
                            add_res("toki%d" % pb, xsem)

                        def f(te, xsem=xsem, xr_ld=xr_ld):
                            wt(te, xsem, xr_ld)
                            use_res(te, "ps_tx")
                            ins = None
                            for u in range(2):
                                for j in range(8):
                                    c0 = u * 1024 + j * 128
                                    ins = te.transpose(
                                        ps_tx[:, c0:c0 + 128],
                                        xrow[:, c0:c0 + 128], ident_sb[:])
                            inc(ins, "pe")
                        TE(f)
                        tx_mm = n("pe")
                        set_res("xrow", "pe")

                        def f(v, s2=s2, tx_mm=tx_mm):
                            wt(v, "pe", tx_mm)
                            if s2 == 0:
                                use_res(v, "xg")
                            ins = None
                            for u in range(2):
                                s = 2 * s2 + u
                                ov = xg.rearrange("p (j w) -> p j w", w=512)[
                                    :, :, s * 128:(s + 1) * 128]
                                iv = ps_tx[:, u * 1024:(u + 1) * 1024]\
                                    .rearrange("p (j w) -> p j w", w=128)
                                ins = v.tensor_copy(ov, iv)
                            inc(ins, "dve")
                        VE(f)
                        set_res("ps_tx", "dve")
                    xg_done = n("dve")
                    flush_scat()

                    for hc in range(32):
                        hb = hc % 2

                        def f(te, hc=hc, hb=hb, xg_done=xg_done):
                            if hc == 0:
                                wt(te, "wld", w_done)
                                wt(te, "wbf", wbf_done)
                                wt(te, "dve", xg_done)
                                te.wait_ge(sems["dve"], o5_done)
                            use_res(te, "ps_h%d" % hb)
                            te.matmul(ps_h[:, hb * 512:(hb + 1) * 512],
                                      b1r_sb[0:1, hc * 128:(hc + 1) * 128],
                                      ones5[0:1, :], start=True, stop=False)
                            ins = None
                            for j in range(8):
                                ins = te.matmul(
                                    ps_h[:, hb * 512:(hb + 1) * 512],
                                    w1_sb[:, j * HDN + hc * 128:
                                          j * HDN + (hc + 1) * 128],
                                    xg[:, j * 512:(j + 1) * 512],
                                    start=False, stop=(j == 7))
                            inc(ins, "pe")
                        TE(f)
                        h_mm = n("pe")

                        def f(sc, hc=hc, hb=hb, h_mm=h_mm):
                            wt(sc, "pe", h_mm)
                            inc(sc.activation(
                                hT_g[:, hc * 512:(hc + 1) * 512],
                                ps_h[:, hb * 512:(hb + 1) * 512],
                                AF.Relu), "act")
                        SC(f)
                        set_res("ps_h%d" % hb, "act")
                    set_res("xg", "pe")
                    h_done = n("act")

                    for oc in range(8):
                        ob = oc % 2

                        def f(te, oc=oc, ob=ob, h_done=h_done):
                            if oc == 0:
                                wt(te, "act", h_done)
                            use_res(te, "ps_y%d" % ob)
                            te.matmul(ps_y[:, ob * 512:(ob + 1) * 512],
                                      b2r_sb[0:1, oc * 128:(oc + 1) * 128],
                                      ones5[0:1, :], start=True, stop=False)
                            ins = None
                            for hcc in range(32):
                                ins = te.matmul(
                                    ps_y[:, ob * 512:(ob + 1) * 512],
                                    w2_sb[:, hcc * 1024 + oc * 128:
                                          hcc * 1024 + (oc + 1) * 128],
                                    hT_g[:, hcc * 512:(hcc + 1) * 512],
                                    start=False, stop=(hcc == 31))
                            inc(ins, "pe")
                        TE(f)
                        y_mm = n("pe")

                        def f(v, oc=oc, ob=ob, y_mm=y_mm):
                            wt(v, "pe", y_mm)
                            if oc == 0:
                                use_res(v, "yT")
                            inc(v.tensor_copy(
                                yT_sb[:, oc * 512:(oc + 1) * 512],
                                ps_y[:, ob * 512:(ob + 1) * 512]), "dve")
                        VE(f)
                        set_res("ps_y%d" % ob, "dve")
                    yT_done = n("dve")

                    for s in range(4):
                        pb2 = s % 2

                        def f(te, s=s, pb2=pb2, yT_done=yT_done):
                            if s == 0:
                                wt(te, "dve", yT_done)
                            use_res(te, "ps_ty%d" % pb2)
                            ins = None
                            for oc in range(8):
                                ins = te.transpose(
                                    ps_ty[:, pb2 * 1024 + oc * 128:
                                          pb2 * 1024 + (oc + 1) * 128],
                                    yT_sb[:, oc * 512 + s * 128:
                                          oc * 512 + (s + 1) * 128],
                                    ident_sb[:])
                            inc(ins, "pe")
                        TE(f)
                        ty_mm = n("pe")

                        def f(v, s=s, pb2=pb2, ty_mm=ty_mm, tg_done=tg_done):
                            wt(v, "pe", ty_mm)
                            wt(v, "dve", tg_done)
                            use_res(v, "yr%d" % s)
                            inc(v.tensor_scalar(
                                yr_sb[:, s * 1024:(s + 1) * 1024],
                                ps_ty[:, pb2 * 1024:(pb2 + 1) * 1024],
                                gat[:, pb * 4 + s:pb * 4 + s + 1], None,
                                op0=ALU.mult), "dve")
                        VE(f)
                        yr_ev = n("dve")
                        set_res("ps_ty%d" % pb2, "dve")

                        def f(g, s=s, yr_ev=yr_ev, gg=gg,
                              ct_done=ct_done, pb=pb):
                            wt(g, "dve", yr_ev)
                            if gg == 0 and s == 0:
                                wt(g, "ycz")
                            wt(g, "ct%d" % pb, ct_done)
                            inc(g.indirect_dma_start(
                                ycon,
                                IndirectOffsetOnAxis(
                                    dsti[:, pb * 4 + s:pb * 4 + s + 1], 0),
                                yr_sb[:, s * 1024:(s + 1) * 1024], None),
                                "ym%d" % (s % 2), 16)
                        pend_scat.append((f, s, pb))
                    set_res("yT", "pe")
                flush_scat()

                def f(g):
                    wt(g, "ym0")
                    wt(g, "ym1")
                    inc(g.collective_compute(
                        "ReduceScatter", ALU.add,
                        replica_groups=[list(range(NCORES))],
                        ins=[ycon[0:8 * NTOK, :].opt()],
                        outs=[yrs.opt()]), "ccy")
                GP(f)

        join_all()
        last_use.clear()

        with ExitStack() as s4v:
            if phases >= 4 and MOE_V1:
                e4 = s4v.enter_context
                xf = e4(nc.sbuf_tensor("xf", [128, 2 * 1024], F32))
                af = e4(nc.sbuf_tensor("af", [128, 2 * 1024], BF16))
                yf = e4(nc.sbuf_tensor("yf", [128, 2 * 1024], BF16))
                of = e4(nc.sbuf_tensor("of", [128, 2 * 1024], F32))

                for gt in range(64):
                    bi = gt % 2

                    def f(sy, gt=gt, bi=bi):
                        use_res(sy, "xa%d" % bi)
                        sl = slice(bi * 1024, (bi + 1) * 1024)
                        rw = slice(gt * 128, (gt + 1) * 128)
                        inc(sy.dma_start(out=xf[:, sl], in_=x_io[rw, :]),
                            "ldf%d" % bi, 16)
                        inc(sy.dma_start(out=af[:, sl], in_=attn_sc[rw, :]),
                            "ldf%d" % bi, 16)
                        inc(sy.dma_start(out=yf[:, sl], in_=yrs[rw, :]),
                            "ldf%d" % bi, 16)
                    SY(f)
                    xa_ld = n("ldf%d" % bi)

                    def f(v, gt=gt, bi=bi, xa_ld=xa_ld):
                        wt(v, "ldf%d" % bi, xa_ld)
                        use_res(v, "of%d" % bi)
                        sl = slice(bi * 1024, (bi + 1) * 1024)
                        cs(v, v.tensor_tensor(of[:, sl], xf[:, sl],
                                              af[:, sl], op=ALU.add))
                        inc(v.tensor_tensor(of[:, sl], of[:, sl],
                                            yf[:, sl], op=ALU.add), "dve")
                    VE(f)
                    o_ev = n("dve")
                    set_res("xa%d" % bi, "dve")

                    def f(g, gt=gt, bi=bi, o_ev=o_ev):
                        wt(g, "dve", o_ev)
                        inc(g.dma_start(
                            out=out_io[gt * 128:(gt + 1) * 128, :],
                            in_=of[:, bi * 1024:(bi + 1) * 1024]),
                            "out%d" % bi, 16)
                    GP(f)
                    set_res("of%d" % bi, "out%d" % bi)

        with ExitStack() as s3:
            if phases >= 3 and not MOE_V1:
                e3 = s3.enter_context
                w1_sb = e3(nc.sbuf_tensor("w1_sb", [128, 8 * HDN], BF16))
                w2_sb = e3(nc.sbuf_tensor("w2_sb", [128, 32 * 1024], BF16))
                b1_sb = e3(nc.sbuf_tensor("b1_sb", [128, 32], F32))
                b2_sb = e3(nc.sbuf_tensor("b2_sb", [128, 8], F32))
                xng = e3(nc.sbuf_tensor("xng", [128, 2 * 8 * 512], BF16))
                hT_g = e3(nc.sbuf_tensor("hT_g", [128, 32 * 512], BF16))
                cb_sb = e3(nc.sbuf_tensor("cb_sb", [128, 2 * 512], F32))
                cr_sb = e3(nc.sbuf_tensor("cr_sb", [1, 2 * 512], BF16))
                y_sb = e3(nc.sbuf_tensor("y_sb", [128, 2 * 512], BF16))
                ps_h = e3(nc.psum_tensor("ps_h", [128, 1024], F32))
                ps_y = e3(nc.psum_tensor("ps_y", [128, 1024], F32))
                ps_cb = e3(nc.psum_tensor("ps_cb", [128, 512], F32))

                crow_flat = crow_sc.rearrange("e a b -> e (a b)")

                for ee in range(E):
                    def f(g):
                        use_res(g, "w12")
                        inc(g.dma_start(out=w1_sb[:], in_=w1_io[ee]), "cst", 16)
                        inc(g.dma_start(out=w2_sb[:], in_=w2_io[ee]), "cst", 16)
                        inc(g.dma_start(out=b1_sb[:], in_=b1_io[ee]), "cst", 16)
                        inc(g.dma_start(out=b2_sb[:], in_=b2_io[ee]), "cst", 16)
                    GP(f)
                    w_done = n("cst")

                    for g8 in range(16):
                        bi = g8 % 2

                        def f(sy):
                            use_res(sy, "xng%d" % bi, "cr%d" % bi)
                            for j in range(8):
                                inc(sy.dma_start(out=xng[:, (bi * 8 + j) * 512:
                                                         (bi * 8 + j + 1) * 512],
                                                 in_=xnT_sc[j, g8]), "ld", 16)
                            inc(sy.dma_start(out=cr_sb[:, bi * 512:(bi + 1) * 512],
                                             in_=crow_flat[ee:ee + 1,
                                                           g8 * 512:(g8 + 1) * 512]),
                                "ld", 16)
                        SY(f)
                        xg_ld = n("ld")

                        def f(te):
                            wt(te, "cst", w_done)
                            wt(te, "ld", xg_ld)
                            use_res(te, "ps_cb")
                            inc(te.matmul(ps_cb[:], ones_r[:],
                                          cr_sb[:, bi * 512:(bi + 1) * 512],
                                          start=True, stop=True), "pe")
                        TE(f)
                        cb_mm = n("pe")

                        def f(v):
                            wt(v, "pe", cb_mm)
                            use_res(v, "cb%d" % bi)
                            inc(v.tensor_copy(cb_sb[:, bi * 512:(bi + 1) * 512],
                                              ps_cb[:]), "dve")
                            cnts["vch"] += 1
                            v.nop().then_inc(sems["vch"], 1)
                        VE(f)
                        cb_done = n("dve")
                        cb_sync = n("vch")
                        set_res("ps_cb", "dve")

                        for hc in range(32):
                            hb = hc % 2

                            def f(te):
                                use_res(te, "ps_h%d" % hb)
                                for j in range(8):
                                    ins = te.matmul(
                                        ps_h[:, hb * 512:(hb + 1) * 512],
                                        w1_sb[:, j * HDN + hc * 128:
                                              j * HDN + (hc + 1) * 128],
                                        xng[:, (bi * 8 + j) * 512:
                                            (bi * 8 + j + 1) * 512],
                                        start=(j == 0), stop=(j == 7))
                                inc(ins, "pe")
                            TE(f)
                            h_mm = n("pe")

                            def f(sc):
                                wt(sc, "pe", h_mm)
                                inc(sc.activation(hT_g[:, hc * 512:(hc + 1) * 512],
                                                  ps_h[:, hb * 512:(hb + 1) * 512],
                                                  AF.Relu, bias=b1_sb[:, hc:hc + 1]),
                                    "act")
                            SC(f)
                            set_res("ps_h%d" % hb, "act")
                        h_done = n("act")
                        set_res("xng%d" % bi, "pe")
                        set_res("cr%d" % bi, "pe")

                        for oc in range(8):
                            ob = oc % 2

                            def f(te):
                                wt(te, "act", h_done)
                                use_res(te, "ps_y%d" % ob)
                                for hc in range(32):
                                    ins = te.matmul(
                                        ps_y[:, ob * 512:(ob + 1) * 512],
                                        w2_sb[:, hc * 1024 + oc * 128:
                                              hc * 1024 + (oc + 1) * 128],
                                        hT_g[:, hc * 512:(hc + 1) * 512],
                                        start=(hc == 0), stop=(hc == 31))
                                inc(ins, "pe")
                            TE(f)
                            y_mm = n("pe")

                            def f(v):
                                wt(v, "pe", y_mm)
                                wt(v, "dve", cb_done)
                                use_res(v, "y%d" % ob)
                                inc(v.scalar_tensor_tensor(
                                    y_sb[:, ob * 512:ob * 512 + 512],
                                    ps_y[:, ob * 512:(ob + 1) * 512],
                                    b2_sb[:, oc:oc + 1],
                                    cb_sb[:, bi * 512:(bi + 1) * 512],
                                    op0=ALU.add, op1=ALU.mult), "dve")
                            VE(f)
                            y_ev = n("dve")
                            set_res("ps_y%d" % ob, "dve")

                            def f(g):
                                wt(g, "dve", y_ev)
                                inc(g.dma_start(out=ffn_sc[oc, g8],
                                                in_=y_sb[:, ob * 512:ob * 512 + 512],
                                                accum_op=(ALU.add if ee > 0
                                                          else ALU.bypass)),
                                    "sto", 16)
                            GP(f)
                            set_res("y%d" % ob, "sto")
                    set_res("w12", "pe")

        join_all()
        last_use.clear()

        with ExitStack() as s4:
            if phases >= 4 and not MOE_V1:
                e4 = s4.enter_context
                xf = e4(nc.sbuf_tensor("xf", [128, 2 * 1024], F32))
                af = e4(nc.sbuf_tensor("af", [128, 2 * 1024], BF16))
                ff = e4(nc.sbuf_tensor("ff", [128, 2 * 8 * 512], BF16))
                fr = e4(nc.sbuf_tensor("fr", [128, 2 * 1024], BF16))
                of = e4(nc.sbuf_tensor("of", [128, 2 * 1024], F32))
                ps_tf = e4(nc.psum_tensor("ps_tf", [128, 1024], BF16))

                for g8 in range(16):
                    bg = g8 % 2

                    def f(sy):
                        use_res(sy, "ff%d" % bg)
                        for oc in range(8):
                            inc(sy.dma_start(out=ff[:, (bg * 8 + oc) * 512:
                                                     (bg * 8 + oc + 1) * 512],
                                             in_=ffn_sc[oc, g8]), "ld", 16)
                    SY(f)
                    ff_ld = n("ld")

                    for tq in range(4):
                        gt = g8 * 4 + tq
                        bi = gt % 2

                        def f(sy):
                            use_res(sy, "xa%d" % bi)
                            inc(sy.dma_start(out=xf[:, bi * 1024:(bi + 1) * 1024],
                                             in_=x_io[gt * 128:(gt + 1) * 128, :]),
                                "ld", 16)
                            inc(sy.dma_start(out=af[:, bi * 1024:(bi + 1) * 1024],
                                             in_=attn_sc[gt * 128:(gt + 1) * 128, :]),
                                "ld", 16)
                        SY(f)
                        xa_ld = n("ld")

                        def f(te):
                            wt(te, "ld", ff_ld)
                            use_res(te, "ps_tf")
                            for oc in range(8):
                                ins = te.transpose(
                                    ps_tf[:, oc * 128:(oc + 1) * 128],
                                    ff[:, (bg * 8 + oc) * 512 + tq * 128:
                                       (bg * 8 + oc) * 512 + tq * 128 + 128],
                                    ident_sb[:])
                            inc(ins, "pe")
                        TE(f)
                        ftp = n("pe")
                        if tq == 3:
                            set_res("ff%d" % bg, "pe")

                        def f(v):
                            wt(v, "pe", ftp)
                            use_res(v, "fr%d" % bi)
                            inc(v.tensor_copy(fr[:, bi * 1024:(bi + 1) * 1024],
                                              ps_tf[:]), "dve")
                        VE(f)
                        set_res("ps_tf", "dve")

                        def f(v):
                            wt(v, "ld", xa_ld)
                            v.tensor_tensor(of[:, bi * 1024:(bi + 1) * 1024],
                                            xf[:, bi * 1024:(bi + 1) * 1024],
                                            af[:, bi * 1024:(bi + 1) * 1024],
                                            op=ALU.add)
                            inc(v.tensor_tensor(of[:, bi * 1024:(bi + 1) * 1024],
                                                of[:, bi * 1024:(bi + 1) * 1024],
                                                fr[:, bi * 1024:(bi + 1) * 1024],
                                                op=ALU.add), "dve")
                        VE(f)
                        o_ev = n("dve")
                        set_res("fr%d" % bi, "dve")
                        set_res("xa%d" % bi, "dve")

                        def f(g):
                            wt(g, "dve", o_ev)
                            inc(g.dma_start(out=out_io[gt * 128:(gt + 1) * 128, :],
                                            in_=of[:, bi * 1024:(bi + 1) * 1024]),
                                "outs", 16)
                        GP(f)

        if d_attn is not None:
            def f(g):
                for i in range(16):
                    inc(g.dma_start(out=d_attn[i * 512:(i + 1) * 512, :],
                                    in_=attn_sc[i * 512:(i + 1) * 512, :]),
                        "outs", 16)
            GP(f)
        if d_xn is not None:
            def f(g):
                dv = d_xn.rearrange("(c g p) w -> c g p w", c=8, g=16)
                for c in range(8):
                    for g8 in range(16):
                        inc(g.dma_start(out=dv[c, g8], in_=xnT_sc[c, g8]),
                            "outs", 16)
            GP(f)

        def f(g):
            wt(g, "outs")
            wt(g, "out0")
            wt(g, "out1")
        GP(f)

    return nc, dbg_outs


def _prep_inputs(inputs):
    Wq = np.asarray(inputs["L0_attn_Wq"], np.float32)
    Wk = np.asarray(inputs["L0_attn_Wk"], np.float32)
    Wv = np.asarray(inputs["L0_attn_Wv"], np.float32)
    Wo = np.asarray(inputs["L0_attn_Wo"], np.float32)
    wg = np.asarray(inputs["L0_ffn_wg"], np.float32)
    bq = np.asarray(inputs["L0_attn_bq"], np.float32)
    bk = np.asarray(inputs["L0_attn_bk"], np.float32)
    bv = np.asarray(inputs["L0_attn_bv"], np.float32)
    bo = np.asarray(inputs["L0_attn_bo"], np.float32)
    w1 = np.asarray(inputs["L0_ffn_w1"], np.float32)
    b1 = np.asarray(inputs["L0_ffn_b1"], np.float32)
    w2 = np.asarray(inputs["L0_ffn_w2"], np.float32)
    b2 = np.asarray(inputs["L0_ffn_b2"], np.float32)
    x = np.asarray(inputs["L0_attn_x"], np.float32)

    com = {}
    com["wq"] = np.ascontiguousarray(
        Wq.reshape(8, 128, 1024).transpose(1, 0, 2).reshape(128, 8192))
    kperm = np.concatenate([np.arange(0, 32, 2), np.arange(1, 32, 2),
                            np.arange(32, 64, 2), np.arange(33, 64, 2)])
    Wk = Wk[:, kperm]
    bk = bk[kperm]
    wkvg = np.concatenate([Wk, Wv, wg], axis=1)
    com["wkvg"] = np.ascontiguousarray(
        wkvg.reshape(8, 128, 136).transpose(1, 0, 2).reshape(128, 8 * 136))
    com["wo"] = np.ascontiguousarray(
        Wo.reshape(8, 128, 1024).transpose(1, 0, 2).reshape(128, 8192))
    com["bq"] = bq.reshape(1, 1024)
    com["bkv"] = np.concatenate([bk, bv, np.zeros(8, np.float32)]).reshape(1, 136)
    com["bo"] = np.ascontiguousarray(bo.reshape(8, 128).T)
    com["w1"] = np.ascontiguousarray(
        w1.reshape(E, 8, 128, HDN).transpose(0, 2, 1, 3).reshape(E, 128, 8 * HDN))
    com["w2"] = np.ascontiguousarray(
        w2.reshape(E, 32, 128, 1024).transpose(0, 2, 1, 3).reshape(E, 128, 32 * 1024))
    com["b1"] = np.ascontiguousarray(b1.reshape(E, 32, 128).transpose(0, 2, 1))
    com["b2"] = np.ascontiguousarray(b2.reshape(E, 8, 128).transpose(0, 2, 1))

    half = HD // 2
    theta = (1.0 / 10000.0 ** (np.arange(0, half, 2, dtype=np.float64) / half))
    hpos = np.arange(NH, dtype=np.float64)
    hc = np.cos(hpos[:, None] * theta[None, :])
    hs = np.sin(hpos[:, None] * theta[None, :])
    A = np.zeros((NH, 64)); Bc = np.zeros((NH, 64))
    A[:, 0:16] = hc; Bc[:, 0:16] = -hs
    A[:, 16:32] = hs; Bc[:, 16:32] = hc
    A[:, 32:48] = hc; Bc[:, 32:48] = -hs
    A[:, 48:64] = hs; Bc[:, 48:64] = hc
    com["cosT"] = np.broadcast_to(A.reshape(1, 1024), (128, 1024)
                                  ).astype(np.float32).copy()
    com["sinT"] = np.broadcast_to(Bc.reshape(1, 1024), (128, 1024)
                                  ).astype(np.float32).copy()
    com["identc"] = np.eye(128, dtype=np.float32)
    tri = np.zeros((128, 4, 512), np.float32)
    ts = np.arange(128)[:, None]
    tq = np.arange(512)[None, :]
    for j in range(4):
        tri[:, j, :] = np.where(128 * j + ts > tq, NEG, 0.0)
    com["tri"] = tri.reshape(128, 2048)
    com["ltri"] = (np.arange(128)[:, None] < np.arange(128)[None, :]
                   ).astype(np.float32)
    selz = np.zeros((16, 8, 128), np.float32)
    for c in range(8):
        selz[2 * c, c, 0:64] = 1.0
        selz[2 * c + 1, c, 64:128] = 1.0
    com["selz"] = selz.reshape(16, 1024)
    com["iotar"] = np.broadcast_to(
        (8.0 - np.arange(8, dtype=np.float32)), (128, 8)).copy()

    if MOE_V1:
        com_w1, com_w2 = com.pop("w1"), com.pop("w2")
        com_b1, com_b2 = com.pop("b1"), com.pop("b2")
    p_ = np.arange(128)[:, None]
    maps = []
    for c in range(NCORES):
        m = dict(com)
        m["x"] = np.ascontiguousarray(x[4 * c:4 * c + 4].reshape(NTOK, D))
        if MOE_V1:
            m["w1"] = com_w1[c:c + 1]
            m["w2"] = com_w2[c:c + 1]
            m["b1"] = com_b1[c:c + 1]
            m["b2"] = com_b2[c:c + 1]
            m["b1r"] = b1[c].reshape(1, HDN)
            m["b2r"] = b2[c].reshape(1, D)
            m["gidx"] = (c * NTOK + 128 * np.arange(64)[None, :] + p_
                         ).astype(np.float32)
            m["ebase"] = (c * ESLOT + 128 * np.arange(128)[None, :] + p_
                          ).astype(np.int32)
        selb = np.zeros((128, 16), np.float32)
        for ee in range(E):
            for cp in range(c):
                selb[cp * 16 + ee, ee] = 1.0
            for cp in range(NCORES):
                selb[cp * 16 + ee, 8 + ee] = 1.0
            for cp in range(c):
                selb[cp * 16 + 8 + ee, 8 + ee] = 1.0
        m["selb"] = selb
        maps.append(m)
    return maps


_CACHED = {}


def kernel(**inputs):
    dbg = _CACHED.get("dbg", ())
    key = ("nc", tuple(sorted(dbg)))
    if key not in _CACHED:
        _CACHED[key] = build_program(dbg)
    nc, dbg_outs = _CACHED[key]
    maps = _prep_inputs(inputs)
    res = run_bass_kernel_spmd(nc, maps, core_ids=list(range(NCORES)))
    outs = [r["out"] for r in res.results]
    full = np.concatenate(outs, axis=0).reshape(B, T, D)
    if dbg:
        _CACHED["last_results"] = res.results
    return full
